# version 13
# baseline (speedup 1.0000x reference)
import numpy as np
import ml_dtypes
import concourse.bass as bass
import concourse.mybir as mybir
from concourse.bass_utils import run_bass_kernel_spmd

F32 = mybir.dt.float32
BF16 = mybir.dt.bfloat16
U16 = mybir.dt.uint16
AF = mybir.ActivationFunctionType
ALU = mybir.AluOpType
AX = mybir.AxisListType.X

H = 4096
DS = 4096
ST = 128
NH = 64
HD = 64
KC = 4
EPS = 1e-5
PROJ = 8512
TM = 1024
HALO = 3
TMH = TM + HALO
Q = 128
NCH = TM // Q
NMAC = 4
NCORES = 8
FC = 32
NCOL = 67
COLW = [128] * 66 + [64]
COLOFF = np.cumsum([0] + [32 * w for w in COLW]).tolist()
W1TOT = COLOFF[-1]
T3 = [(0, 343), (343, 342), (685, 342)]
NOC = 8
OCW = 512
NKQ = 4


def _pack_rows(v, nchunk):
    out = np.zeros((128, nchunk), np.float32)
    v = np.asarray(v, np.float32).reshape(-1)
    for c in range(nchunk):
        seg = v[c * 128:(c + 1) * 128]
        out[:len(seg), c] = seg
    return out


def host_core_inputs(inp, core):
    b, h = core // 2, core % 2
    xb = np.asarray(inp["L0_attn_hidden_states"][b], np.float32)
    zeros3 = np.zeros((HALO, H), np.float32)
    mac_rows = []
    for mac in range(NMAC):
        if mac < 2:
            if h == 0:
                mac_rows.append(np.zeros((TMH, H), np.float32))
                continue
            t0 = mac * TM
        else:
            t0 = 2048 * h + (mac - 2) * TM
        halo = zeros3 if t0 == 0 else xb[t0 - HALO:t0]
        mac_rows.append(np.concatenate([halo, xb[t0:t0 + TM]], 0))
    return dict(xs4=np.ascontiguousarray(np.stack(mac_rows, 0), dtype=np.float32))


def host_shared(inp):
    W1 = np.asarray(inp["L0_attn_in_proj_w"], np.float32)
    W2 = np.asarray(inp["L0_attn_out_proj_w"], np.float32)
    WT = np.ascontiguousarray(np.asarray(inp["L0_ffn_weight"], np.float32).T)
    w1r = W1.reshape(FC, 128, PROJ)
    blocks = []
    for m in range(NCOL):
        blk = w1r[:, :, 128 * m:128 * m + COLW[m]]
        blocks.append(blk.transpose(1, 0, 2).reshape(128, 32 * COLW[m]))
    w1b = np.ascontiguousarray(np.concatenate(blocks, axis=1))

    def blk_oc(W):
        r = W.reshape(NKQ, 8, 128, NOC, OCW)
        r = r.transpose(3, 0, 2, 1, 4)
        return np.ascontiguousarray(r.reshape(NOC * NKQ * 128, 8 * OCW))

    return dict(
        w1b=w1b, w2b=blk_oc(W2), wtb=blk_oc(WT),
        gwb=np.ascontiguousarray(np.asarray(inp["L0_ffn_gate_w"], np.float32).T
                                 .reshape(FC * 128, 8)),
        lnp=_pack_rows(inp["L0_attn_ln_w"], FC),
        nwp=_pack_rows(inp["L0_attn_norm_w"], FC),
        mupp=_pack_rows(inp["L0_attn_mup_vector"], NCOL),
        cwp=np.ascontiguousarray(np.asarray(inp["L0_attn_conv_w"], np.float32)
                                 .reshape(34, 128, KC).transpose(1, 0, 2).reshape(128, 34 * KC)),
        dtb=np.asarray(inp["L0_attn_dt_bias"], np.float32).reshape(64, 1),
        alog=np.asarray(inp["L0_attn_A_log"], np.float32).reshape(64, 1),
        drow=np.asarray(inp["L0_attn_D"], np.float32).reshape(1, 64),
        icsb=np.asarray(inp["L0_ffn_in_channel_scale"], np.float32),
        ocsb=np.asarray(inp["L0_ffn_out_channel_scale"], np.float32),
        biasr=np.asarray(inp["L0_ffn_bias"], np.float32).reshape(1, H),
        identb=np.eye(128, dtype=ml_dtypes.bfloat16),
        identf=np.eye(128, dtype=np.float32),
        trib=np.triu(np.ones((128, 128), np.float32)).astype(ml_dtypes.bfloat16),
        onesf=np.ones((128, 128), np.float32),
        onesrf=np.ones((1, 128), np.float32),
        onesrb=np.ones((1, 128), dtype=ml_dtypes.bfloat16),
        onescb=np.ones((128, 1), dtype=ml_dtypes.bfloat16),
    )


class Prog:

    def __init__(self, nc):
        self.nc = nc
        self.ops = []
        self.sems = {}
        self.ctxs = []
        self.last = {}

    def sem(self, name):
        c = self.nc.semaphore(name)
        self.sems[name] = [c.__enter__(), 0]
        self.ctxs.append(c)
        return name

    def h(self, s):
        return self.sems[s][0]

    def val(self, s):
        return self.sems[s][1]

    def emit(self, eng, fn, incable=True):
        self.ops.append([eng, fn, []])
        if incable:
            self.last[eng] = len(self.ops) - 1

    def dma(self, q, out, in_, s, **kw):
        self.sems[s][1] += 16
        hh = self.h(s)
        self.emit(q, lambda e: e.dma_start(out=out, in_=in_, **kw).then_inc(hh, 16))
        return self.sems[s][1]

    def tick(self, eng, s):
        hh = self.h(s)
        idx = self.last.get(eng)
        assert idx is not None, f"tick on {eng} with no prior op"
        incs = self.ops[idx][2]
        if incs:
            assert incs[-1][0] is hh, f"mixed-sem double tick on {eng}"
            return self.sems[s][1]
        self.sems[s][1] += 1
        incs.append((hh, 1))
        return self.sems[s][1]

    def selfe(self, eng):
        v = self.tick(eng, self.CANON[eng])
        self.wait(eng, self.CANON[eng], v)

    def wait(self, eng, s, v=None):
        vv = self.val(s) if v is None else v
        hh = self.h(s)
        self.emit(eng, lambda e: e.wait_ge(hh, vv), incable=False)

    ENGS = ("sync", "gpsimd", "vector", "scalar", "tensor")
    DMAQ = ("sync", "gpsimd")
    DSEMS = ("sx", "sw", "so", "sg")
    CANON = {"vector": "sv", "scalar": "sa", "tensor": "sp"}

    def barrier(self, engs=None):
        engs = engs or self.ENGS
        comp = [e for e in engs if e not in self.DMAQ]
        marks = {e: self.tick(e, self.CANON[e]) for e in comp}
        dvals = {s: self.val(s) for s in self.DSEMS}
        for e in engs:
            for o in comp:
                if o != e:
                    self.wait(e, self.CANON[o], marks[o])
            if e not in self.DMAQ:
                for s in self.DSEMS:
                    self.wait(e, s, dvals[s])

    class _Rec:

        def __init__(self, e):
            object.__setattr__(self, "_e", e)
            object.__setattr__(self, "last", None)

        def __getattr__(self, n):
            a = getattr(object.__getattribute__(self, "_e"), n)
            if callable(a):
                def w(*args, **kw):
                    r = a(*args, **kw)
                    if hasattr(r, "then_inc"):
                        object.__setattr__(self, "last", r)
                    return r
                return w
            return a

    def replay(self, eng, e):
        rec = Prog._Rec(e)
        for ee, fn, incs in self.ops:
            if ee != eng:
                continue
            fn(rec)
            r = rec.last
            for hh, n in incs:
                assert r is not None, "tick with no producing instruction"
                r = r.then_inc(hh, n)


def build(n_mac=NMAC, modes=None, taps=(), stop=None):
    if modes is None:
        modes = ["prefix", "prefix", "main", "main"][:n_mac]
    taps = set(taps)
    nc = bass.Bass("TRN2", target_bir_lowering=False, detect_race_conditions=False)

    D = {}
    def din(name, shape, dt):
        D[name] = nc.dram_tensor(name, shape, dt, kind="ExternalInput")
        return D[name]

    xs4 = din("xs4", [NMAC, TMH, H], F32)
    w1b = din("w1b", [128, W1TOT], F32)
    w2b = din("w2b", [NOC * NKQ * 128, 8 * OCW], F32)
    wtb = din("wtb", [NOC * NKQ * 128, 8 * OCW], F32)
    gwb = din("gwb", [FC * 128, 8], F32)
    lnp_d = din("lnp", [128, FC], F32)
    nwp_d = din("nwp", [128, FC], F32)
    mupp_d = din("mupp", [128, NCOL], F32)
    cwp_d = din("cwp", [128, 34 * KC], F32)
    dtb_d = din("dtb", [64, 1], F32)
    alog_d = din("alog", [64, 1], F32)
    drow_d = din("drow", [1, 64], F32)
    icsb_d = din("icsb", [8, H], F32)
    ocsb_d = din("ocsb", [8, H], F32)
    biasr_d = din("biasr", [1, H], F32)
    identb_d = din("identb", [128, 128], BF16)
    identf_d = din("identf", [128, 128], F32)
    trib_d = din("trib", [128, 128], BF16)
    onesf_d = din("onesf", [128, 128], F32)
    onesrf_d = din("onesrf", [1, 128], F32)
    onesrb_d = din("onesrb", [1, 128], BF16)
    onescb_d = din("onescb", [128, 1], BF16)
    out_d = nc.dram_tensor("out", [2 * TM, H], F32, kind="ExternalOutput")

    tap_d = {}
    TAPSHAPES = {
        "xnfm": ([128, FC * TMH], BF16), "r2rep": ([128, TMH], F32),
        "arena": ([128, 34 * TMH], BF16), "dtf": ([64, TMH], F32),
        "dtp": ([64, TMH], F32), "hbuf": ([128, DS], F32),
        "ytm": ([128, DS], BF16), "routfm": ([8, TM], BF16),
        "r3row": ([1, TM], F32),
        "colA": ([128, 8], F32),
        "rowss": ([1, 1024], F32),
    }
    for t in taps:
        base = t.split("@")[0]
        tap_d[t] = nc.dram_tensor("tap_" + t.replace("@", "_"), TAPSHAPES[base][0],
                                  TAPSHAPES[base][1], kind="ExternalOutput")

    from contextlib import ExitStack
    with ExitStack() as X:
        E = X.enter_context

        xnfm = E(nc.sbuf_tensor("xnfm", [128, FC * TMH], BF16))
        arena = E(nc.sbuf_tensor("arena", [128, 34 * TMH], BF16))
        dtfb = E(nc.sbuf_tensor("dtfb", [64, TMH], F32))
        dtpb = E(nc.sbuf_tensor("dtpb", [64, TMH], F32))
        r2rep = E(nc.sbuf_tensor("r2rep", [128, TMH], F32))
        routfm = E(nc.sbuf_tensor("routfm", [8, TM], BF16))
        hbuf = E(nc.sbuf_tensor("hbuf", [128, DS], F32))
        lnp = E(nc.sbuf_tensor("lnp_s", [128, FC], F32))
        lnrec = E(nc.sbuf_tensor("lnrec", [128, FC], F32))
        nwp = E(nc.sbuf_tensor("nwp_s", [128, FC], F32))
        mupp = E(nc.sbuf_tensor("mupp_s", [128, NCOL], F32))
        cwp = E(nc.sbuf_tensor("cwp_s", [128, 34 * KC], F32))
        dtb = E(nc.sbuf_tensor("dtb_s", [64, 1], F32))
        nega = E(nc.sbuf_tensor("nega_s", [64, 1], F32))
        dbc = E(nc.sbuf_tensor("dbc_s", [128, 64], F32))
        ident = E(nc.sbuf_tensor("ident_s", [128, 128], BF16))
        identf = E(nc.sbuf_tensor("identf_s", [128, 128], F32))
        tri = E(nc.sbuf_tensor("tri_s", [128, 128], BF16))
        onesf = E(nc.sbuf_tensor("onesf_s", [128, 128], F32))
        onesrf = E(nc.sbuf_tensor("onesrf_s", [1, 128], F32))
        onesrb = E(nc.sbuf_tensor("onesrb_s", [1, 128], BF16))
        onescb = E(nc.sbuf_tensor("onescb_s", [128, 1], BF16))
        gwr = E(nc.sbuf_tensor("gwr", [128, FC * 8], BF16))
        bias_sl = E(nc.sbuf_tensor("bias_sl", [1, OCW], BF16))
        rows_s = E(nc.sbuf_tensor("rows_s", [1, 1024], F32))
        colA = E(nc.sbuf_tensor("colA", [128, 12], F32))
        catm = E(nc.sbuf_tensor("catm", [128, 64], F32))
        dttm = E(nc.sbuf_tensor("dttm", [128, 64], F32))
        datm = E(nc.sbuf_tensor("datm", [128, 64], F32))
        katm = E(nc.sbuf_tensor("katm", [128, 64], F32))
        atbc = E(nc.sbuf_tensor("atbc", [128, 64], F32))
        ectm = E(nc.sbuf_tensor("ectm", [128, 64], F32))
        cafm = E(nc.sbuf_tensor("cafm", [64, Q], F32))
        bmtm = E(nc.sbuf_tensor("bmtm", [128, 128], BF16))
        mmask = E(nc.sbuf_tensor("mmask", [128, 128], BF16))
        xstm = E(nc.sbuf_tensor("xstm", [128, 512], BF16))
        ytm = E(nc.sbuf_tensor("ytm", [128, 512], BF16))
        scf0 = E(nc.sbuf_tensor("scf0", [128, 512], F32))
        scf1 = E(nc.sbuf_tensor("scf1", [128, 512], F32))
        scf2 = E(nc.sbuf_tensor("scf2", [128, 512], F32))
        sqmid = E(nc.sbuf_tensor("sqmid", [128, 512], F32))
        scb0 = E(nc.sbuf_tensor("scb0", [128, 512], BF16))
        scb1 = E(nc.sbuf_tensor("scb1", [128, 512], BF16))
        scb2 = E(nc.sbuf_tensor("scb2", [128, 512], BF16))
        pha = E(nc.sbuf_tensor("pha", [128, 4096], F32))
        xqA = pha[:, 0:2048]
        xnbv = pha.bitcast(BF16)[:, 4096:8192]
        wslA = pha.bitcast(BF16)[:, 0:FC * 128]
        wslB = pha.bitcast(BF16)[:, FC * 128:2 * FC * 128]
        wcsA = pha.bitcast(BF16)[:, 0:4096]
        wcsB = pha.bitcast(BF16)[:, 4096:8192]
        icsv = pha.bitcast(BF16)[0:8, 4096:8192]
        ocs_sl = E(nc.sbuf_tensor("ocs_sl", [8, OCW], BF16))

        pA = E(nc.psum_tensor("pA", [128, 512], F32))
        pB = E(nc.psum_tensor("pB", [128, 512], F32))
        pT = E(nc.psum_tensor("pT", [128, 1024], BF16))
        pY = E(nc.psum_tensor("pY", [128, 512], F32))
        pS = E(nc.psum_tensor("pS", [128, 512], F32))
        pM = E(nc.psum_tensor("pM", [128, 512], F32))
        pU = E(nc.psum_tensor("pU", [128, 512], F32))
        pV = E(nc.psum_tensor("pV", [128, 512], F32))

        p = Prog(nc)
        for s in ("sx", "sw", "so", "sg", "sv", "sa", "sp",
                  "swc0", "swc1", "swc2", "swc3"):
            p.sem(s)
        for e in Prog.ENGS:
            p.sem("b_" + e)

        def TAP(t, src_ap):
            if t not in taps:
                return
            p.barrier()
            p.dma("sync", tap_d[t][:, :], src_ap, "so")
            p.wait("sync", "so")
            p.barrier()

        for dst, src in ((lnp, lnp_d), (nwp, nwp_d), (mupp, mupp_d), (cwp, cwp_d),
                         (dtb, dtb_d), (ident, identb_d), (identf, identf_d),
                         (tri, trib_d), (onesf, onesf_d), (onesrf, onesrf_d),
                         (onesrb, onesrb_d), (onescb, onescb_d)):
            p.dma("sync", dst[:, :], src[:, :], "sx")
        p.dma("sync", colA[0:64, 0:1], alog_d[:, :], "sx")
        p.dma("sync", rows_s[0:1, 0:64], drow_d[:, :], "sx")
        p.dma("gpsimd", gwr[:, :], gwb.rearrange("(k p) c -> p k c", p=128), "sw")
        p.wait("scalar", "sx")
        p.emit("scalar", lambda sc: sc.activation(nega[:, :], colA[0:64, 0:1], AF.Exp))
        va = p.tick("scalar", "sa")
        p.wait("vector", "sx")
        p.wait("vector", "sw")
        p.wait("vector", "sa", va)
        def init_v(ve):
            ve.tensor_scalar(nega[:, :], nega[:, :], 1e30, -1.0, ALU.min, ALU.mult)
            ve.reciprocal(lnrec[:, :], lnp[:, :])
            ve.memset(hbuf[:, :], 0.0)
        p.emit("vector", init_v)
        p.selfe("vector")
        for k in range(FC):
            p.emit("vector", lambda ve, k=k: ve.tensor_scalar(
                gwr[:, 8 * k:8 * k + 8], gwr[:, 8 * k:8 * k + 8],
                lnrec[:, k:k + 1], None, ALU.mult))
        vv = p.tick("vector", "sv")
        p.wait("tensor", "sv", vv)
        p.emit("tensor", lambda te: te.matmul(pU[:, 0:64], onesrf[:, :],
                                              rows_s[0:1, 0:64], start=True, stop=True))
        vt = p.tick("tensor", "sp")
        p.wait("vector", "sp", vt)
        p.emit("vector", lambda ve: ve.tensor_copy(dbc[:, :], pU[:, 0:64]))
        p.tick("vector", "sv")
        p.barrier()

        def emit_xcopy(mac, r0):
            for i in range(4):
                p.dma("sync", out_d[r0 + 256 * i:r0 + 256 * (i + 1), :],
                      xs4[mac, HALO + 256 * i:HALO + 256 * (i + 1), :], "so")

        def emit_ln(mac):
            p.barrier()
            tiles = [(128 * i, 128) for i in range(8)] + [(1024, 3)]
            for (g0, rn) in tiles:
                p.wait("scalar", "sp")
                for half in range(2):
                    vh = p.dma("sync", xqA[0:rn, :], xs4[mac, g0:g0 + rn, 2048 * half:2048 * (half + 1)], "sx")
                    p.wait("vector", "sx", vh)
                    p.wait("scalar", "sx", vh)
                    def redc(ve, rn=rn, half=half):
                        ve.reduce_sum(colA[0:rn, half:half + 1], xqA[0:rn, :], axis=AX)
                        ve.tensor_copy(xnbv[0:rn, 2048 * half:2048 * (half + 1)], xqA[0:rn, :])
                    p.emit("vector", redc)
                    p.selfe("vector")
                    vv = p.tick("vector", "sv")
                    p.wait("scalar", "sv", vv)
                    for k4 in range(4):
                        p.emit("scalar", lambda sc, rn=rn, half=half, k4=k4: sc.activation(
                            pV[0:rn, :], xqA[0:rn, 512 * k4:512 * (k4 + 1)], AF.Square,
                            accum_out=colA[0:rn, 2 + 4 * half + k4:3 + 4 * half + k4]))
                    va = p.tick("scalar", "sa")
                    p.wait("sync", "sa", va)
                    p.wait("sync", "sv", vv)
                p.wait("vector", "sa")
                for st_ in [
                    lambda ve, rn=rn: ve.tensor_tensor(colA[0:rn, 0:1], colA[0:rn, 0:1], colA[0:rn, 1:2], ALU.add),
                    lambda ve, rn=rn: ve.tensor_scalar(colA[0:rn, 0:1], colA[0:rn, 0:1], 1.0 / H, None, ALU.mult),
                    lambda ve, rn=rn: ve.reduce_sum(colA[0:rn, 3:4], colA[0:rn, 2:10], axis=AX),
                    lambda ve, rn=rn: ve.tensor_scalar(colA[0:rn, 3:4], colA[0:rn, 3:4], 1.0 / H, None, ALU.mult),
                    lambda ve, rn=rn: ve.tensor_tensor(colA[0:rn, 4:5], colA[0:rn, 0:1], colA[0:rn, 0:1], ALU.mult),
                    lambda ve, rn=rn: ve.tensor_tensor(colA[0:rn, 3:4], colA[0:rn, 3:4], colA[0:rn, 4:5], ALU.subtract),
                    lambda ve, rn=rn: ve.tensor_scalar(colA[0:rn, 4:5], colA[0:rn, 3:4], EPS, None, ALU.add),
                    lambda ve, rn=rn: ve.reciprocal(colA[0:rn, 5:6], colA[0:rn, 4:5]),
                    lambda ve, rn=rn: ve.tensor_tensor(colA[0:rn, 5:6], colA[0:rn, 3:4], colA[0:rn, 5:6], ALU.mult),
                    lambda ve, rn=rn: ve.tensor_scalar(colA[0:rn, 5:6], colA[0:rn, 5:6], EPS, None, ALU.add),
                ]:
                    p.emit("vector", st_)
                    p.selfe("vector")
                vv = p.tick("vector", "sv")
                p.wait("scalar", "sv", vv)
                p.emit("scalar", lambda sc, rn=rn: sc.activation(colA[0:rn, 4:5], colA[0:rn, 4:5], AF.Sqrt))
                p.emit("scalar", lambda sc, rn=rn: sc.activation(colA[0:rn, 5:6], colA[0:rn, 5:6], AF.Sqrt))
                va = p.tick("scalar", "sa")
                p.wait("vector", "sa", va)
                p.emit("vector", lambda ve, rn=rn: ve.reciprocal(colA[0:rn, 4:5], colA[0:rn, 4:5]))
                p.selfe("vector")
                p.emit("vector", lambda ve, rn=rn: ve.reciprocal(colA[0:rn, 5:6], colA[0:rn, 5:6]))
                p.selfe("vector")
                def xnb_(ve, rn=rn):
                    ve.tensor_scalar(xnbv[0:rn, :], xnbv[0:rn, :], colA[0:rn, 0:1], None, ALU.subtract)
                    ve.tensor_scalar(xnbv[0:rn, :], xnbv[0:rn, :], colA[0:rn, 4:5], None, ALU.mult)
                p.emit("vector", xnb_)
                vv = p.tick("vector", "sv")
                p.wait("tensor", "sv", vv)
                p.emit("tensor", lambda te, rn=rn: te.transpose(
                    pM[0:1, 0:rn], colA[0:rn, 5:6], identf[0:rn, 0:rn]))
                vt = p.tick("tensor", "sp")
                p.wait("vector", "sp", vt)
                p.emit("vector", lambda ve, rn=rn: ve.tensor_copy(rows_s[0:1, 0:rn], pM[0:1, 0:rn]))
                vv = p.tick("vector", "sv")
                p.wait("tensor", "sv", vv)
                p.emit("tensor", lambda te, rn=rn: te.matmul(
                    pU[:, 0:rn], onesrf[:, :], rows_s[0:1, 0:rn], start=True, stop=True))
                for g in range(4):
                    for j in range(8):
                        fc = 8 * g + j
                        p.emit("tensor", lambda te, fc=fc, j=j, rn=rn: te.transpose(
                            pT[:, 128 * j:128 * j + rn],
                            xnbv[0:rn, 128 * fc:128 * fc + 128], ident[0:rn, 0:rn]))
                    vt = p.tick("tensor", "sp")
                    p.wait("vector", "sp", vt)
                    for j in range(8):
                        fc = 8 * g + j
                        p.emit("vector", lambda ve, fc=fc, j=j, rn=rn, g0=g0: ve.tensor_scalar(
                            xnfm[:, TMH * fc + g0:TMH * fc + g0 + rn],
                            pT[:, 128 * j:128 * j + rn], lnp[:, fc:fc + 1], None, ALU.mult))
                    vv = p.tick("vector", "sv")
                    p.wait("tensor", "sv", vv)
                p.emit("vector", lambda ve, rn=rn, g0=g0: ve.tensor_copy(
                    r2rep[:, g0:g0 + rn], pU[:, 0:rn]))
                vv = p.tick("vector", "sv")
                p.wait("sync", "sv", vv)
                p.wait("scalar", "sv", vv)

        def emit_router(mac):
            for tt in range(8):
                g0 = HALO + 128 * tt
                for k in range(FC):
                    p.emit("tensor", lambda te, k=k, g0=g0: te.matmul(
                        pA[:, 0:8], xnfm[:, TMH * k + g0:TMH * k + g0 + 128],
                        gwr[:, 8 * k:8 * k + 8], start=(k == 0), stop=(k == FC - 1)))
                vt = p.tick("tensor", "sp")
                p.wait("vector", "sp", vt)
                p.emit("vector", lambda ve: ve.reduce_max(colA[:, 0:1], pA[:, 0:8], axis=AX))
                p.selfe("vector")
                p.emit("vector", lambda ve: ve.tensor_scalar(scf0[:, 0:8], pA[:, 0:8], colA[:, 0:1], None, ALU.subtract))
                p.selfe("vector")
                vv = p.tick("vector", "sv")
                p.wait("scalar", "sv", vv)
                p.emit("scalar", lambda sc: sc.activation(scf0[:, 0:8], scf0[:, 0:8], AF.Exp))
                va = p.tick("scalar", "sa")
                p.wait("vector", "sa", va)
                p.emit("vector", lambda ve: ve.reduce_sum(colA[:, 1:2], scf0[:, 0:8], axis=AX))
                p.selfe("vector")
                p.emit("vector", lambda ve: ve.reciprocal(colA[:, 1:2], colA[:, 1:2]))
                p.selfe("vector")
                p.emit("vector", lambda ve: ve.tensor_scalar(scb0[:, 0:8], scf0[:, 0:8], colA[:, 1:2], None, ALU.mult))
                p.selfe("vector")
                vv = p.tick("vector", "sv")
                p.wait("tensor", "sv", vv)
                p.emit("tensor", lambda te: te.transpose(pT[0:8, 0:128], scb0[:, 0:8], ident[:, :]))
                vt = p.tick("tensor", "sp")
                p.wait("vector", "sp", vt)
                p.emit("vector", lambda ve, tt=tt: ve.tensor_copy(
                    routfm[:, 128 * tt:128 * tt + 128], pT[0:8, 0:128]))
                vv = p.tick("vector", "sv")
                p.wait("tensor", "sv", vv)

        def emit_p1(mac):
            p.wait("gpsimd", "sv")
            p.wait("gpsimd", "sa")
            p.wait("gpsimd", "sp")
            cols = list(range(32, NCOL))
            lastmm = {}
            evtick = {}
            for i, m in enumerate(cols):
                buf = (wslA, wslB)[i % 2]
                w = COLW[m]
                if i >= 2:
                    p.wait("gpsimd", "sp", lastmm[i % 2])
                vw = p.dma("gpsimd", buf[:, 0:FC * w],
                           w1b[:, COLOFF[m]:COLOFF[m] + FC * w], "sw")
                p.wait("tensor", "sw", vw)
                for ti, (t0, tn) in enumerate(T3):
                    bi = (3 * i + ti) % 3
                    pp = (pA, pB, pV)[bi]
                    if bi in evtick:
                        p.wait("tensor", "sv", evtick[bi])
                    for k in range(FC):
                        p.emit("tensor", lambda te, pp=pp, buf=buf, k=k, w=w, t0=t0, tn=tn: te.matmul(
                            pp[0:w, 0:tn], buf[:, w * k:w * k + w],
                            xnfm[:, TMH * k + t0:TMH * k + t0 + tn],
                            start=(k == 0), stop=(k == FC - 1)))
                    vt = p.tick("tensor", "sp")
                    p.wait("vector", "sp", vt)
                    if m < 66:
                        cc = m - 32
                        p.emit("vector", lambda ve, pp=pp, cc=cc, m=m, t0=t0, tn=tn: ve.scalar_tensor_tensor(
                            arena[:, TMH * cc + t0:TMH * cc + t0 + tn], pp[0:128, 0:tn],
                            mupp[:, m:m + 1], r2rep[:, t0:t0 + tn], ALU.mult, ALU.mult))
                    else:
                        p.emit("vector", lambda ve, pp=pp, t0=t0, tn=tn: ve.scalar_tensor_tensor(
                            dtfb[:, t0:t0 + tn], pp[0:64, 0:tn],
                            mupp[0:64, 66:67], r2rep[0:64, t0:t0 + tn], ALU.mult, ALU.mult))
                    evtick[bi] = p.tick("vector", "sv")
                lastmm[i % 2] = p.val("sp")
            dtf = dtfb[:, :]
            dtp = dtpb[:, :]
            def sp1(ve):
                ve.tensor_scalar(dtf, dtf, dtb[:, :], None, ALU.add)
                ve.tensor_scalar(dtp, dtf, 0.0, None, ALU.max)
            p.emit("vector", sp1)
            vv = p.tick("vector", "sv")
            p.wait("scalar", "sv", vv)
            p.emit("scalar", lambda sc: sc.activation(dtf, dtf, AF.Abs))
            p.emit("scalar", lambda sc: sc.activation(dtf, dtf, AF.Exp, scale=-1.0))
            p.emit("scalar", lambda sc: sc.activation(dtf, dtf, AF.Ln, bias=1.0))
            va = p.tick("scalar", "sa")
            p.wait("vector", "sa", va)
            def sp2(ve):
                ve.tensor_tensor(dtp, dtp, dtf, ALU.add)
                ve.tensor_scalar(dtf, dtp, nega[:, :], None, ALU.mult)
            p.emit("vector", sp2)
            p.tick("vector", "sv")

        def emit_conv(mac):
            p.barrier(("vector", "scalar", "tensor"))
            silu_tick = {}
            for c in range(34):
                b0, b1 = (scf0, scf1) if c % 2 == 0 else (scf2, sqmid)
                if c >= 2:
                    p.wait("vector", "sa", silu_tick[c % 2])
                def cv(ve, c=c, b0=b0, b1=b1):
                    ve.tensor_scalar(b0[:, 0:512], arena[:, TMH * c:TMH * c + 512],
                                     cwp[:, 4 * c:4 * c + 1], None, ALU.mult)
                    ve.tensor_scalar(b1[:, 0:512], arena[:, TMH * c + 512:TMH * c + 1024],
                                     cwp[:, 4 * c:4 * c + 1], None, ALU.mult)
                    for k in (1, 2, 3):
                        ve.scalar_tensor_tensor(b0[:, 0:512], arena[:, TMH * c + k:TMH * c + k + 512],
                                                cwp[:, 4 * c + k:4 * c + k + 1], b0[:, 0:512],
                                                ALU.mult, ALU.add)
                        ve.scalar_tensor_tensor(b1[:, 0:512], arena[:, TMH * c + 512 + k:TMH * c + 512 + k + 512],
                                                cwp[:, 4 * c + k:4 * c + k + 1], b1[:, 0:512],
                                                ALU.mult, ALU.add)
                p.emit("vector", cv)
                vv = p.tick("vector", "sv")
                p.wait("scalar", "sv", vv)
                p.emit("scalar", lambda sc, c=c, b0=b0: sc.activation(
                    arena[:, TMH * c:TMH * c + 512], b0[:, 0:512], AF.Silu))
                p.emit("scalar", lambda sc, c=c, b1=b1: sc.activation(
                    arena[:, TMH * c + 512:TMH * c + 1024], b1[:, 0:512], AF.Silu))
                silu_tick[c % 2] = p.tick("scalar", "sa")
            p.wait("vector", "sa")

        def emit_scan(mac, prefix):
            dtf = dtfb[:, :]
            dtp = dtpb[:, :]
            p.barrier(("vector", "scalar", "tensor"))
            sqb = sqmid.bitcast(BF16)
            vcopy = t_yintra = t_smm = a_exp = a_yevscale = v_yevTT = None
            for q in range(NCH):
                t0 = q * Q
                d0 = HALO + q * Q
                p.emit("vector", lambda ve, d0=d0: ve.tensor_tensor_scan(
                    cafm[:, :], dtf[:, d0:d0 + Q], dtf[:, d0:d0 + Q], 0.0, ALU.add, ALU.bypass))
                vv = p.tick("vector", "sv")
                p.wait("tensor", "sv", vv)
                p.emit("tensor", lambda te: te.transpose(pM[:, 0:64], cafm[:, :], identf[0:64, 0:64]))
                p.emit("tensor", lambda te, d0=d0: te.transpose(pM[:, 64:128], dtf[:, d0:d0 + Q], identf[0:64, 0:64]))
                p.emit("tensor", lambda te, d0=d0: te.transpose(pM[:, 128:192], dtp[:, d0:d0 + Q], identf[0:64, 0:64]))
                p.emit("tensor", lambda te, t0=t0: te.transpose(
                    pT[:, 0:128], arena[:, TMH * 32 + t0:TMH * 32 + t0 + Q], ident[:, :]))
                vt = p.tick("tensor", "sp")
                p.wait("vector", "sp", vt)
                def ev_t(ve):
                    ve.tensor_copy(catm[:, :], pM[:, 0:64])
                    ve.tensor_copy(datm[:, :], pM[:, 64:128])
                    ve.tensor_copy(dttm[:, :], pM[:, 128:192])
                    ve.tensor_copy(bmtm[:, :], pT[:, 0:128])
                p.emit("vector", ev_t)
                vv = p.tick("vector", "sv")
                p.wait("tensor", "sv", vv)
                if a_yevscale is not None:
                    p.wait("tensor", "sa", a_yevscale)
                p.emit("tensor", lambda te: te.matmul(pU[:, 0:64], onesf[:, :], datm[:, :],
                                                      start=True, stop=True))
                vt = p.tick("tensor", "sp")
                p.wait("vector", "sp", vt)
                p.emit("vector", lambda ve: ve.tensor_tensor(katm[:, :], pU[:, 0:64], catm[:, :], ALU.subtract))
                vv = p.tick("vector", "sv")
                p.wait("scalar", "sv", vv)
                p.emit("scalar", lambda sc: sc.activation(atbc[:, :], pU[:, 0:64], AF.Exp))
                p.emit("scalar", lambda sc: sc.activation(katm[:, :], katm[:, :], AF.Exp))
                if not prefix:
                    p.emit("scalar", lambda sc: sc.activation(ectm[:, :], catm[:, :], AF.Exp))
                va = p.tick("scalar", "sa")
                p.wait("vector", "sa", va)
                p.wait("tensor", "sa", va)
                if not prefix:
                    p.emit("tensor", lambda te, t0=t0: te.matmul(
                        pM[:, 0:128], arena[:, TMH * 32 + t0:TMH * 32 + t0 + Q],
                        arena[:, TMH * 33 + t0:TMH * 33 + t0 + Q], start=True, stop=True))
                    vt = p.tick("tensor", "sp")
                    p.wait("vector", "sp", vt)
                    p.emit("vector", lambda ve: ve.tensor_tensor(mmask[:, :], pM[:, 0:128], tri[:, :], ALU.mult))
                    vv = p.tick("vector", "sv")
                    p.wait("tensor", "sv", vv)
                for g in range(8):
                    if vcopy is not None:
                        p.wait("tensor", "sv", vcopy)
                    for j in range(4):
                        c = 4 * g + j
                        p.emit("tensor", lambda te, c=c, j=j, t0=t0: te.transpose(
                            pT[:, 128 * j:128 * (j + 1)],
                            arena[:, TMH * c + t0:TMH * c + t0 + Q], ident[:, :]))
                    vt = p.tick("tensor", "sp")
                    if t_yintra is not None:
                        p.wait("vector", "sp", t_yintra)
                    elif t_smm is not None:
                        p.wait("vector", "sp", t_smm)
                    p.wait("vector", "sp", vt)
                    p.emit("vector", lambda ve: ve.tensor_copy(xstm[:, :], pT[:, 0:512]))
                    if not prefix:
                        p.emit("vector", lambda ve, g=g: ve.tensor_copy(
                            scb2[:, :], hbuf[:, 512 * g:512 * (g + 1)]))
                    for j in range(8):
                        r = 8 * g + j
                        p.emit("vector", lambda ve, r=r, j=j: ve.tensor_scalar(
                            scb0[:, 64 * j:64 * (j + 1)], xstm[:, 64 * j:64 * (j + 1)],
                            dttm[:, r:r + 1], None, ALU.mult))
                    vcopy = p.tick("vector", "sv")
                    p.wait("tensor", "sv", vcopy)
                    if not prefix:
                        if a_exp is not None:
                            p.wait("vector", "sa", a_exp)
                        def dg8(ve, g=g):
                            for jj in range(4):
                                r = 8 * g + jj
                                ve.tensor_scalar(scf0[:, 128 * jj:128 * (jj + 1)],
                                                 identf[:, :], catm[:, r:r + 1], None, ALU.mult)
                            for jj in range(4):
                                r = 8 * g + 4 + jj
                                ve.tensor_scalar(scf1[:, 128 * jj:128 * (jj + 1)],
                                                 identf[:, :], catm[:, r:r + 1], None, ALU.mult)
                        p.emit("vector", dg8)
                        v_dg = p.tick("vector", "sv")
                        p.emit("tensor", lambda te, g=g, t0=t0: te.matmul(
                            pU[:, :], arena[:, TMH * 33 + t0:TMH * 33 + t0 + Q],
                            scb2[:, :], start=True, stop=True))
                        vys = p.tick("tensor", "sp")
                        p.wait("tensor", "sv", v_dg)
                        p.emit("tensor", lambda te: te.matmul(pB[:, :], onesf[:, :],
                                                              scf0[:, :], start=True, stop=True))
                        p.emit("tensor", lambda te: te.matmul(pV[:, :], onesf[:, :],
                                                              scf1[:, :], start=True, stop=True))
                        t_rr = p.tick("tensor", "sp")
                        p.wait("scalar", "sv", vcopy)
                        p.wait("scalar", "sp", vys)
                        def xw8(sc, g=g):
                            for j in range(8):
                                r = 8 * g + j
                                sc.activation(scb2[:, 64 * j:64 * (j + 1)],
                                              scb0[:, 64 * j:64 * (j + 1)], AF.Copy,
                                              scale=katm[:, r:r + 1])
                        p.emit("scalar", xw8)
                        a_xw = p.tick("scalar", "sa")
                        if v_yevTT is not None:
                            p.wait("scalar", "sv", v_yevTT)
                        def ys8(sc, g=g):
                            for j in range(8):
                                r = 8 * g + j
                                sc.activation(scf2[:, 64 * j:64 * (j + 1)],
                                              pU[:, 64 * j:64 * (j + 1)], AF.Copy,
                                              scale=ectm[:, r:r + 1])
                        p.emit("scalar", ys8)
                        a_yevscale = p.tick("scalar", "sa")
                        p.wait("tensor", "sa", a_xw)
                        p.emit("tensor", lambda te: te.matmul(pS[:, :], bmtm[:, :], scb2[:, :],
                                                              start=True, stop=True))
                        t_smm = p.tick("tensor", "sp")
                        p.wait("vector", "sp", t_rr)
                        def g28(ve, g=g):
                            for jj in range(4):
                                r = 8 * g + jj
                                ve.tensor_scalar(scf0[:, 128 * jj:128 * (jj + 1)],
                                                 pB[:, 128 * jj:128 * (jj + 1)],
                                                 catm[:, r:r + 1], 0.0, ALU.subtract, ALU.min)
                            for jj in range(4):
                                r = 8 * g + 4 + jj
                                ve.tensor_scalar(scf1[:, 128 * jj:128 * (jj + 1)],
                                                 pV[:, 128 * jj:128 * (jj + 1)],
                                                 catm[:, r:r + 1], 0.0, ALU.subtract, ALU.min)
                        p.emit("vector", g28)
                        v_g2 = p.tick("vector", "sv")
                        p.wait("scalar", "sv", v_g2)
                        if t_yintra is not None:
                            p.wait("scalar", "sp", t_yintra)
                        p.emit("scalar", lambda sc: sc.activation(sqb[:, 0:512], scf0[:, :], AF.Exp))
                        p.emit("scalar", lambda sc: sc.activation(sqb[:, 512:1024], scf1[:, :], AF.Exp))
                        a_exp = p.tick("scalar", "sa")
                        p.wait("vector", "sa", a_exp)
                        def gmb2(ve):
                            ve.tensor_tensor(
                                sqb[:, 0:512].rearrange("p (a b) -> p a b", a=4),
                                sqb[:, 0:512].rearrange("p (a b) -> p a b", a=4),
                                mmask[:, None, :].broadcast_to([128, 4, 128]), ALU.mult)
                            ve.tensor_tensor(
                                sqb[:, 512:1024].rearrange("p (a b) -> p a b", a=4),
                                sqb[:, 512:1024].rearrange("p (a b) -> p a b", a=4),
                                mmask[:, None, :].broadcast_to([128, 4, 128]), ALU.mult)
                        p.emit("vector", gmb2)
                        v_gmb = p.tick("vector", "sv")
                        p.wait("tensor", "sv", v_gmb)
                        for j in range(8):
                            p.emit("tensor", lambda te, j=j: te.matmul(
                                pY[:, 64 * j:64 * (j + 1)], sqb[:, 128 * j:128 * (j + 1)],
                                scb0[:, 64 * j:64 * (j + 1)],
                                start=(j == 0), stop=(j == 7)))
                        t_yintra = p.tick("tensor", "sp")
                    else:
                        p.wait("scalar", "sv", vcopy)
                        if t_smm is not None:
                            p.wait("scalar", "sp", t_smm)
                        def xw8(sc, g=g):
                            for j in range(8):
                                r = 8 * g + j
                                sc.activation(scb2[:, 64 * j:64 * (j + 1)],
                                              scb0[:, 64 * j:64 * (j + 1)], AF.Copy,
                                              scale=katm[:, r:r + 1])
                        p.emit("scalar", xw8)
                        a_xw = p.tick("scalar", "sa")
                        p.wait("tensor", "sa", a_xw)
                        p.emit("tensor", lambda te: te.matmul(pS[:, :], bmtm[:, :], scb2[:, :],
                                                              start=True, stop=True))
                        t_smm = p.tick("tensor", "sp")
                    p.wait("vector", "sp", t_smm)
                    def hup(ve, g=g):
                        for j in range(8):
                            r = 8 * g + j
                            ve.scalar_tensor_tensor(hbuf[:, 64 * r:64 * (r + 1)],
                                                    hbuf[:, 64 * r:64 * (r + 1)],
                                                    atbc[:, r:r + 1], pS[:, 64 * j:64 * (j + 1)],
                                                    ALU.mult, ALU.add)
                    p.emit("vector", hup)
                    if not prefix:
                        p.wait("vector", "sa", a_yevscale)
                        def yev(ve, g=g):
                            for j in range(8):
                                r = 8 * g + j
                                ve.scalar_tensor_tensor(scf2[:, 64 * j:64 * (j + 1)],
                                                        xstm[:, 64 * j:64 * (j + 1)],
                                                        dbc[:, r:r + 1], scf2[:, 64 * j:64 * (j + 1)],
                                                        ALU.mult, ALU.add)
                        p.emit("vector", yev)
                        p.wait("vector", "sp", t_yintra)
                        p.emit("vector", lambda ve: ve.tensor_tensor(ytm[:, :], pY[:, :], scf2[:, :], ALU.add))
                        v_yevTT = p.tick("vector", "sv")
                        p.wait("tensor", "sv", v_yevTT)
                        pMb = pM.bitcast(BF16)
                        for j in range(4):
                            p.emit("tensor", lambda te, j=j, pMb=pMb: te.transpose(
                                pMb[:, 128 * j:128 * (j + 1)], ytm[:, 128 * j:128 * (j + 1)],
                                ident[:, :]))
                        t_ytr = p.tick("tensor", "sp")
                        p.wait("vector", "sp", t_ytr)
                        def yst(ve, g=g, t0=t0, pMb=pMb):
                            for j in range(4):
                                c = 4 * g + j
                                ve.tensor_copy(arena[:, TMH * c + t0:TMH * c + t0 + Q],
                                               pMb[:, 128 * j:128 * (j + 1)])
                        p.emit("vector", yst)
                    p.tick("vector", "sv")

        def emit_z(mac):
            p.barrier(("vector", "scalar", "tensor", "gpsimd"))
            lastmm = {}
            evtick = {}
            for m in range(32):
                buf = (wslA, wslB)[m % 2]
                if m >= 2:
                    p.wait("gpsimd", "sp", lastmm[m % 2])
                vw = p.dma("gpsimd", buf[:, 0:FC * 128], w1b[:, COLOFF[m]:COLOFF[m] + FC * 128], "sw")
                p.wait("tensor", "sw", vw)
                for tt in range(2):
                    t0 = HALO + 512 * tt
                    bi = (2 * m + tt) % 2
                    pp = (pA, pB)[bi]
                    par = (2 * m + tt) % 2
                    sf0 = (scf0, scf1)[par]
                    sq_ = (scb0, scb2)[par]
                    if bi in evtick:
                        p.wait("tensor", "sv", evtick[bi])
                    for k in range(FC):
                        p.emit("tensor", lambda te, pp=pp, buf=buf, k=k, t0=t0: te.matmul(
                            pp[:, :], buf[:, 128 * k:128 * (k + 1)],
                            xnfm[:, TMH * k + t0:TMH * k + t0 + 512],
                            start=(k == 0), stop=(k == FC - 1)))
                    vt = p.tick("tensor", "sp")
                    p.wait("vector", "sp", vt)
                    p.emit("vector", lambda ve, pp=pp, m=m, t0=t0, sf0=sf0: ve.scalar_tensor_tensor(
                        sf0[:, :], pp[:, :], mupp[:, m:m + 1], r2rep[:, t0:t0 + 512],
                        ALU.mult, ALU.mult))
                    evtick[bi] = p.tick("vector", "sv")
                    p.wait("scalar", "sv", evtick[bi])
                    p.emit("scalar", lambda sc, sf0=sf0: sc.activation(
                        scf2[:, :], sf0[:, :], AF.Silu))
                    va = p.tick("scalar", "sa")
                    p.wait("vector", "sa", va)
                    lt = 512 * tt
                    p.emit("vector", lambda ve, m=m, lt=lt: ve.tensor_tensor(
                        arena[:, TMH * m + lt:TMH * m + lt + 512],
                        arena[:, TMH * m + lt:TMH * m + lt + 512], scf2[:, :], ALU.mult))
                    p.tick("vector", "sv")
                lastmm[m % 2] = p.val("sp")

        def emit_rstd3(mac):
            p.barrier(("vector", "scalar", "tensor"))
            sqticks = {}
            for c in range(FC):
                for tt in range(2):
                    lt = 512 * tt
                    sq_ = (scb0, scb2)[(2 * c + tt) % 2]
                    if (c, tt) not in ((0, 0), (0, 1)):
                        p.wait("scalar", "sp", sqticks[(2 * c + tt) % 2])
                    p.emit("scalar", lambda sc, c=c, lt=lt, sq_=sq_: sc.activation(
                        sq_[:, :], arena[:, TMH * c + lt:TMH * c + lt + 512], AF.Square))
                    va = p.tick("scalar", "sa")
                    p.wait("tensor", "sa", va)
                    pst = (pU, pV)[tt]
                    p.emit("tensor", lambda te, pst=pst, c=c, sq_=sq_: te.matmul(
                        pst[0:1, :], onescb[:, :], sq_[:, :],
                        start=(c == 0), stop=(c == FC - 1)))
                    sqticks[(2 * c + tt) % 2] = p.tick("tensor", "sp")
            p.wait("vector", "sp")
            def r3a(ve):
                ve.tensor_copy(rows_s[0:1, 0:512], pU[0:1, :])
                ve.tensor_copy(rows_s[0:1, 512:1024], pV[0:1, :])
                ve.tensor_scalar(rows_s[0:1, 0:1024], rows_s[0:1, 0:1024], 1.0 / DS, EPS,
                                 ALU.mult, ALU.add)
            p.emit("vector", r3a)
            vv = p.tick("vector", "sv")
            p.wait("scalar", "sv", vv)
            p.emit("scalar", lambda sc: sc.activation(rows_s[0:1, 0:1024], rows_s[0:1, 0:1024], AF.Sqrt))
            va = p.tick("scalar", "sa")
            p.wait("vector", "sa", va)
            p.emit("vector", lambda ve: ve.reciprocal(rows_s[0:1, 0:1024], rows_s[0:1, 0:1024]))
            vv = p.tick("vector", "sv")
            p.wait("tensor", "sv", vv)
            for tt in range(2):
                p.emit("tensor", lambda te, tt=tt: te.matmul(
                    pM[:, :], onesrf[:, :], rows_s[0:1, 512 * tt:512 * (tt + 1)],
                    start=True, stop=True))
                vt = p.tick("tensor", "sp")
                p.wait("vector", "sp", vt)
                p.emit("vector", lambda ve, tt=tt: ve.tensor_copy(
                    r2rep[:, 512 * tt:512 * (tt + 1)], pM[:, :]))
                vv = p.tick("vector", "sv")
                p.wait("tensor", "sv", vv)

        def emit_yn(mac):
            for c in range(FC):
                for lt in (0, 512):
                    p.emit("vector", lambda ve, c=c, lt=lt: ve.scalar_tensor_tensor(
                        arena[:, TMH * c + lt:TMH * c + lt + 512],
                        arena[:, TMH * c + lt:TMH * c + lt + 512],
                        nwp[:, c:c + 1], r2rep[:, lt:lt + 512], ALU.mult, ALU.mult))
            p.tick("vector", "sv")

        def emit_xin(mac):
            p.wait("gpsimd", "sv")
            p.wait("gpsimd", "sp")
            vw = p.dma("gpsimd", icsv, icsb_d[:, :], "sw")
            p.wait("tensor", "sw", vw)
            for fc in range(FC):
                for tt in range(2):
                    pp = (pA, pB)[tt]
                    p.emit("tensor", lambda te, pp=pp, fc=fc, tt=tt: te.matmul(
                        pp[:, :], icsv[:, 128 * fc:128 * (fc + 1)],
                        routfm[:, 512 * tt:512 * (tt + 1)], start=True, stop=True))
                    vt = p.tick("tensor", "sp")
                    p.wait("vector", "sp", vt)
                    g0 = HALO + 512 * tt
                    p.emit("vector", lambda ve, pp=pp, fc=fc, g0=g0: ve.scalar_tensor_tensor(
                        xnfm[:, TMH * fc + g0:TMH * fc + g0 + 512], pp[:, :],
                        lnrec[:, fc:fc + 1], xnfm[:, TMH * fc + g0:TMH * fc + g0 + 512],
                        ALU.mult, ALU.mult))
                    vv = p.tick("vector", "sv")
                    p.wait("tensor", "sv", vv)

        def emit_c(mac, ffn, r0):
            p.barrier(("vector", "scalar", "tensor", "gpsimd"))
            p.wait("gpsimd", "so")
            p.wait("gpsimd", "sg")
            wsrc = wtb if ffn else w2b
            coff = HALO if ffn else 0
            srcbase = xnfm if ffn else arena
            banks = [pA[:, :], pB[:, :], pY[:, :], pS[:, :], pM[:, :], pU[:, :], pV[:, :], pT.bitcast(F32)[:, 0:512]]
            lastuse = {}
            bank_ev = {}
            NSL = NOC * NKQ
            slab_tick = {}
            sign_tick = {}

            def issue_slab(n):
                buf = (wcsA, wcsB)[n % 2]
                if n >= 2:
                    p.wait("gpsimd", "sp", lastuse[n % 2])
                slab_tick[n] = p.dma("gpsimd", buf[:, :],
                                     wsrc[n * 128:(n + 1) * 128, :], "swc%d" % (n % 2))

            def sign_slab(n):
                buf = (wcsA, wcsB)[n % 2]
                p.wait("vector", "swc%d" % (n % 2), slab_tick[n])
                p.emit("vector", lambda ve, buf=buf: ve.tensor_scalar(
                    buf.bitcast(U16), buf.bitcast(U16), 0x8000, 0x3F80,
                    ALU.bitwise_and, ALU.bitwise_or))
                sign_tick[n] = p.tick("vector", "sv")

            for n in range(2):
                issue_slab(n)
                if ffn:
                    sign_slab(n)
            for j in range(NOC):
                if ffn:
                    p.wait("gpsimd", "sp")
                    vw = p.dma("gpsimd", ocs_sl[:, :], ocsb_d[:, OCW * j:OCW * (j + 1)], "sw")
                    vw = p.dma("gpsimd", bias_sl[:, :], biasr_d[0:1, OCW * j:OCW * (j + 1)], "sw")
                    p.wait("tensor", "sw", vw)
                for kq in range(NKQ):
                    n = j * NKQ + kq
                    buf = (wcsA, wcsB)[n % 2]
                    if ffn:
                        p.wait("tensor", "sv", sign_tick[n])
                    else:
                        p.wait("tensor", "swc%d" % (n % 2), slab_tick[n])
                    for tt in range(8):
                        if kq == 0 and tt in bank_ev:
                            p.wait("tensor", bank_ev[tt][0], bank_ev[tt][1])
                        for kin in range(8):
                            k = kq * 8 + kin
                            g0 = coff + 128 * tt
                            p.emit("tensor", lambda te, bank=banks[tt], buf=buf, kin=kin, k=k, g0=g0: te.matmul(
                                bank,
                                srcbase[:, TMH * k + g0:TMH * k + g0 + 128],
                                buf[:, 512 * kin:512 * (kin + 1)],
                                start=(k == 0),
                                stop=(k == 31 and not ffn)))
                    vt = p.tick("tensor", "sp")
                    lastuse[n % 2] = vt
                    if n + 2 < NSL:
                        issue_slab(n + 2)
                        if ffn:
                            sign_slab(n + 2)
                if ffn:
                    for tt in range(8):
                        p.emit("tensor", lambda te, tt=tt: te.matmul(
                            banks[tt], onesrb[:, :], bias_sl[0:1, :],
                            start=False, stop=True))
                vt = p.tick("tensor", "sp")
                stags = (scf0, scf1)
                dmaticks = {}
                for tt in range(8):
                    stag = stags[tt % 2]
                    peng = "scalar" if (tt % 2 == 0 and not ffn) else "vector"
                    if tt >= 2:
                        p.wait(peng, "sg", dmaticks[tt - 2])
                    if tt == 0:
                        p.wait("vector", "sp", vt)
                        p.wait("scalar", "sp", vt)
                    if tt % 2 == 0 and not ffn:
                        p.emit("scalar", lambda sc, tt=tt, stag=stag: sc.activation(
                            stag[:, :], banks[tt], AF.Copy))
                        esem, ev = "sa", p.tick("scalar", "sa")
                    else:
                        p.emit("vector", lambda ve, tt=tt, stag=stag: ve.tensor_copy(stag[:, :], banks[tt]))
                        esem, ev = "sv", p.tick("vector", "sv")
                    bank_ev[tt] = (esem, ev)
                    if ffn:
                        p.wait("tensor", esem, ev)
                        p.emit("tensor", lambda te, tt=tt, j=j: te.matmul(
                            banks[tt], routfm[:, 128 * tt:128 * tt + 128],
                            ocs_sl[:, :], start=True, stop=True))
                        vt2 = p.tick("tensor", "sp")
                        p.wait("vector", "sp", vt2)
                        def fev(ve, tt=tt, stag=stag):
                            ve.tensor_copy(scf2[:, :], banks[tt])
                            ve.tensor_tensor(stag[:, :], stag[:, :], scf2[:, :], ALU.mult)
                        p.emit("vector", fev)
                        esem, ev = "sv", p.tick("vector", "sv")
                        bank_ev[tt] = (esem, ev)
                    p.wait("gpsimd", esem, ev)
                    dmaticks[tt] = p.dma("gpsimd",
                                         out_d[r0 + 128 * tt:r0 + 128 * (tt + 1), OCW * j:OCW * (j + 1)],
                                         stag[:, :], "sg", accum_op=ALU.add)
                p.wait("vector", "sg", dmaticks[6])
                p.wait("vector", "sg", dmaticks[7])
                p.wait("scalar", "sg", dmaticks[6])

        n_main = 0
        for mac in range(n_mac):
            prefix = modes[mac] == "prefix"
            r0 = n_main * TM
            if not prefix:
                n_main += 1
                emit_xcopy(mac, r0)
            emit_ln(mac)
            TAP(f"xnfm@{mac}", xnfm[:, :])
            TAP(f"r2rep@{mac}", r2rep[:, :])
            TAP(f"colA@{mac}", colA[:, :])
            TAP(f"rowss@{mac}", rows_s[:, :])
            if stop == "ln":
                break
            if not prefix:
                emit_router(mac)
                TAP(f"routfm@{mac}", routfm[:, :])
            if stop == "router":
                break
            emit_p1(mac)
            TAP(f"arena@{mac}p1", arena[:, :])
            TAP(f"dtf@{mac}", dtfb[:, :])
            TAP(f"dtp@{mac}", dtpb[:, :])
            if stop == "p1":
                break
            emit_conv(mac)
            TAP(f"arena@{mac}cv", arena[:, :])
            if stop == "conv":
                break
            emit_scan(mac, prefix)
            TAP(f"hbuf@{mac}", hbuf[:, :])
            if stop == "scan":
                break
            if not prefix:
                TAP(f"arena@{mac}y", arena[:, :])
                emit_z(mac)
                emit_rstd3(mac)
                TAP(f"arena@{mac}z", arena[:, :])
                if stop == "z":
                    break
                emit_yn(mac)
                emit_xin(mac)
                if stop == "xin":
                    break
                emit_c(mac, ffn=False, r0=r0)
                if stop == "c1":
                    break
                emit_c(mac, ffn=True, r0=r0)
        p.barrier()
        p.wait("sync", "so")
        p.wait("gpsimd", "so")
        p.wait("gpsimd", "sg")
        p.wait("gpsimd", "sw")

        with nc.Block() as block:
            @block.sync
            def _(e):
                p.replay("sync", e)

            @block.gpsimd
            def _(e):
                p.replay("gpsimd", e)

            @block.vector
            def _(e):
                p.replay("vector", e)

            @block.scalar
            def _(e):
                p.replay("scalar", e)

            @block.tensor
            def _(e):
                p.replay("tensor", e)

    return nc


def kernel(**inputs):
    sh = host_shared(inputs)
    nc = build()
    in_maps = []
    for c in range(NCORES):
        m = dict(sh)
        m.update(host_core_inputs(inputs, c))
        in_maps.append(m)
    res = run_bass_kernel_spmd(nc, in_maps, core_ids=list(range(NCORES)))
    out = np.zeros((4, 4096, H), np.float32)
    for c in range(NCORES):
        b, h = c // 2, c % 2
        out[b, 2048 * h:2048 * (h + 1)] = res.results[c]["out"]
    return out


# version 14
# speedup vs baseline: 1.0140x; 1.0140x over previous
import numpy as np
import ml_dtypes
import concourse.bass as bass
import concourse.mybir as mybir
from concourse.bass_utils import run_bass_kernel_spmd

F32 = mybir.dt.float32
BF16 = mybir.dt.bfloat16
U16 = mybir.dt.uint16
AF = mybir.ActivationFunctionType
ALU = mybir.AluOpType
AX = mybir.AxisListType.X

H = 4096
DS = 4096
ST = 128
NH = 64
HD = 64
KC = 4
EPS = 1e-5
PROJ = 8512
TM = 1024
HALO = 3
TMH = TM + HALO
Q = 128
NCH = TM // Q
NMAC = 4
NCORES = 8
FC = 32
NCOL = 67
COLW = [128] * 66 + [64]
COLOFF = np.cumsum([0] + [32 * w for w in COLW]).tolist()
W1TOT = COLOFF[-1]
T3 = [(0, 343), (343, 342), (685, 342)]
NOC = 8
OCW = 512
NKQ = 4


def _pack_rows(v, nchunk):
    out = np.zeros((128, nchunk), np.float32)
    v = np.asarray(v, np.float32).reshape(-1)
    for c in range(nchunk):
        seg = v[c * 128:(c + 1) * 128]
        out[:len(seg), c] = seg
    return out


def host_core_inputs(inp, core):
    b, h = core // 2, core % 2
    xb = np.asarray(inp["L0_attn_hidden_states"][b], np.float32)
    zeros3 = np.zeros((HALO, H), np.float32)
    mac_rows = []
    for mac in range(NMAC):
        if mac < 2:
            if h == 0:
                mac_rows.append(np.zeros((TMH, H), np.float32))
                continue
            t0 = mac * TM
        else:
            t0 = 2048 * h + (mac - 2) * TM
        halo = zeros3 if t0 == 0 else xb[t0 - HALO:t0]
        mac_rows.append(np.concatenate([halo, xb[t0:t0 + TM]], 0))
    return dict(xs4=np.ascontiguousarray(np.stack(mac_rows, 0), dtype=np.float32))


def host_shared(inp):
    W1 = np.asarray(inp["L0_attn_in_proj_w"], np.float32)
    W2 = np.asarray(inp["L0_attn_out_proj_w"], np.float32)
    WT = np.ascontiguousarray(np.asarray(inp["L0_ffn_weight"], np.float32).T)
    w1r = W1.reshape(FC, 128, PROJ)
    blocks = []
    for m in range(NCOL):
        blk = w1r[:, :, 128 * m:128 * m + COLW[m]]
        blocks.append(blk.transpose(1, 0, 2).reshape(128, 32 * COLW[m]))
    w1b = np.ascontiguousarray(np.concatenate(blocks, axis=1))

    def blk_oc(W):
        r = W.reshape(NKQ, 8, 128, NOC, OCW)
        r = r.transpose(3, 0, 2, 1, 4)
        return np.ascontiguousarray(r.reshape(NOC * NKQ * 128, 8 * OCW))

    return dict(
        w1b=w1b, w2b=blk_oc(W2), wtb=blk_oc(WT),
        gwb=np.ascontiguousarray(np.asarray(inp["L0_ffn_gate_w"], np.float32).T
                                 .reshape(FC * 128, 8)),
        lnp=_pack_rows(inp["L0_attn_ln_w"], FC),
        nwp=_pack_rows(inp["L0_attn_norm_w"], FC),
        mupp=_pack_rows(inp["L0_attn_mup_vector"], NCOL),
        cwp=np.ascontiguousarray(np.asarray(inp["L0_attn_conv_w"], np.float32)
                                 .reshape(34, 128, KC).transpose(1, 0, 2).reshape(128, 34 * KC)),
        dtb=np.asarray(inp["L0_attn_dt_bias"], np.float32).reshape(64, 1),
        alog=np.asarray(inp["L0_attn_A_log"], np.float32).reshape(64, 1),
        drow=np.asarray(inp["L0_attn_D"], np.float32).reshape(1, 64),
        icsb=np.asarray(inp["L0_ffn_in_channel_scale"], np.float32),
        ocsb=np.asarray(inp["L0_ffn_out_channel_scale"], np.float32),
        biasr=np.asarray(inp["L0_ffn_bias"], np.float32).reshape(1, H),
        identb=np.eye(128, dtype=ml_dtypes.bfloat16),
        identf=np.eye(128, dtype=np.float32),
        trib=np.triu(np.ones((128, 128), np.float32)).astype(ml_dtypes.bfloat16),
        onesf=np.ones((128, 128), np.float32),
        onesrf=np.ones((1, 128), np.float32),
        onesrb=np.ones((1, 128), dtype=ml_dtypes.bfloat16),
        onescb=np.ones((128, 1), dtype=ml_dtypes.bfloat16),
    )


class Prog:

    def __init__(self, nc):
        self.nc = nc
        self.ops = []
        self.sems = {}
        self.ctxs = []
        self.last = {}

    def sem(self, name):
        c = self.nc.semaphore(name)
        self.sems[name] = [c.__enter__(), 0]
        self.ctxs.append(c)
        return name

    def h(self, s):
        return self.sems[s][0]

    def val(self, s):
        return self.sems[s][1]

    def emit(self, eng, fn, incable=True):
        self.ops.append([eng, fn, []])
        if incable:
            self.last[eng] = len(self.ops) - 1

    def dma(self, q, out, in_, s, **kw):
        self.sems[s][1] += 16
        hh = self.h(s)
        self.emit(q, lambda e: e.dma_start(out=out, in_=in_, **kw).then_inc(hh, 16))
        return self.sems[s][1]

    def tick(self, eng, s):
        hh = self.h(s)
        idx = self.last.get(eng)
        assert idx is not None, f"tick on {eng} with no prior op"
        incs = self.ops[idx][2]
        if incs:
            assert incs[-1][0] is hh, f"mixed-sem double tick on {eng}"
            return self.sems[s][1]
        self.sems[s][1] += 1
        incs.append((hh, 1))
        return self.sems[s][1]

    def selfe(self, eng):
        v = self.tick(eng, self.CANON[eng])
        self.wait(eng, self.CANON[eng], v)

    def wait(self, eng, s, v=None):
        vv = self.val(s) if v is None else v
        hh = self.h(s)
        self.emit(eng, lambda e: e.wait_ge(hh, vv), incable=False)

    ENGS = ("sync", "gpsimd", "vector", "scalar", "tensor")
    DMAQ = ("sync", "gpsimd")
    DSEMS = ("sx", "sw", "so", "sg")
    CANON = {"vector": "sv", "scalar": "sa", "tensor": "sp"}

    def barrier(self, engs=None):
        engs = engs or self.ENGS
        comp = [e for e in engs if e not in self.DMAQ]
        marks = {e: self.tick(e, self.CANON[e]) for e in comp}
        dvals = {s: self.val(s) for s in self.DSEMS}
        for e in engs:
            for o in comp:
                if o != e:
                    self.wait(e, self.CANON[o], marks[o])
            if e not in self.DMAQ:
                for s in self.DSEMS:
                    self.wait(e, s, dvals[s])

    class _Rec:

        def __init__(self, e):
            object.__setattr__(self, "_e", e)
            object.__setattr__(self, "last", None)

        def __getattr__(self, n):
            a = getattr(object.__getattribute__(self, "_e"), n)
            if callable(a):
                def w(*args, **kw):
                    r = a(*args, **kw)
                    if hasattr(r, "then_inc"):
                        object.__setattr__(self, "last", r)
                    return r
                return w
            return a

    def replay(self, eng, e):
        rec = Prog._Rec(e)
        for ee, fn, incs in self.ops:
            if ee != eng:
                continue
            fn(rec)
            r = rec.last
            for hh, n in incs:
                assert r is not None, "tick with no producing instruction"
                r = r.then_inc(hh, n)


def build(n_mac=NMAC, modes=None, taps=(), stop=None):
    if modes is None:
        modes = ["prefix", "prefix", "main", "main"][:n_mac]
    taps = set(taps)
    nc = bass.Bass("TRN2", target_bir_lowering=False, detect_race_conditions=False)

    D = {}
    def din(name, shape, dt):
        D[name] = nc.dram_tensor(name, shape, dt, kind="ExternalInput")
        return D[name]

    xs4 = din("xs4", [NMAC, TMH, H], F32)
    w1b = din("w1b", [128, W1TOT], F32)
    w2b = din("w2b", [NOC * NKQ * 128, 8 * OCW], F32)
    wtb = din("wtb", [NOC * NKQ * 128, 8 * OCW], F32)
    gwb = din("gwb", [FC * 128, 8], F32)
    lnp_d = din("lnp", [128, FC], F32)
    nwp_d = din("nwp", [128, FC], F32)
    mupp_d = din("mupp", [128, NCOL], F32)
    cwp_d = din("cwp", [128, 34 * KC], F32)
    dtb_d = din("dtb", [64, 1], F32)
    alog_d = din("alog", [64, 1], F32)
    drow_d = din("drow", [1, 64], F32)
    icsb_d = din("icsb", [8, H], F32)
    ocsb_d = din("ocsb", [8, H], F32)
    biasr_d = din("biasr", [1, H], F32)
    identb_d = din("identb", [128, 128], BF16)
    identf_d = din("identf", [128, 128], F32)
    trib_d = din("trib", [128, 128], BF16)
    onesf_d = din("onesf", [128, 128], F32)
    onesrf_d = din("onesrf", [1, 128], F32)
    onesrb_d = din("onesrb", [1, 128], BF16)
    onescb_d = din("onescb", [128, 1], BF16)
    out_d = nc.dram_tensor("out", [2 * TM, H], F32, kind="ExternalOutput")

    tap_d = {}
    TAPSHAPES = {
        "xnfm": ([128, FC * TMH], BF16), "r2rep": ([128, TMH], F32),
        "arena": ([128, 34 * TMH], BF16), "dtf": ([64, TMH], F32),
        "dtp": ([64, TMH], F32), "hbuf": ([128, DS], F32),
        "ytm": ([128, DS], BF16), "routfm": ([8, TM], BF16),
        "r3row": ([1, TM], F32),
        "colA": ([128, 8], F32),
        "rowss": ([1, 1024], F32),
    }
    for t in taps:
        base = t.split("@")[0]
        tap_d[t] = nc.dram_tensor("tap_" + t.replace("@", "_"), TAPSHAPES[base][0],
                                  TAPSHAPES[base][1], kind="ExternalOutput")

    from contextlib import ExitStack
    with ExitStack() as X:
        E = X.enter_context

        xnfm = E(nc.sbuf_tensor("xnfm", [128, FC * TMH], BF16))
        arena = E(nc.sbuf_tensor("arena", [128, 34 * TMH], BF16))
        dtfb = E(nc.sbuf_tensor("dtfb", [64, TMH], F32))
        dtpb = E(nc.sbuf_tensor("dtpb", [64, TMH], F32))
        r2rep = E(nc.sbuf_tensor("r2rep", [128, TMH], F32))
        routfm = E(nc.sbuf_tensor("routfm", [8, TM], BF16))
        hbuf = E(nc.sbuf_tensor("hbuf", [128, DS], F32))
        lnp = E(nc.sbuf_tensor("lnp_s", [128, FC], F32))
        lnrec = E(nc.sbuf_tensor("lnrec", [128, FC], F32))
        nwp = E(nc.sbuf_tensor("nwp_s", [128, FC], F32))
        mupp = E(nc.sbuf_tensor("mupp_s", [128, NCOL], F32))
        cwp = E(nc.sbuf_tensor("cwp_s", [128, 34 * KC], F32))
        dtb = E(nc.sbuf_tensor("dtb_s", [64, 1], F32))
        nega = E(nc.sbuf_tensor("nega_s", [64, 1], F32))
        dbc = E(nc.sbuf_tensor("dbc_s", [128, 64], F32))
        ident = E(nc.sbuf_tensor("ident_s", [128, 128], BF16))
        identf = E(nc.sbuf_tensor("identf_s", [128, 128], F32))
        tri = E(nc.sbuf_tensor("tri_s", [128, 128], BF16))
        onesf = E(nc.sbuf_tensor("onesf_s", [128, 128], F32))
        onesrf = E(nc.sbuf_tensor("onesrf_s", [1, 128], F32))
        onesrb = E(nc.sbuf_tensor("onesrb_s", [1, 128], BF16))
        onescb = E(nc.sbuf_tensor("onescb_s", [128, 1], BF16))
        gwr = E(nc.sbuf_tensor("gwr", [128, FC * 8], BF16))
        bias_sl = E(nc.sbuf_tensor("bias_sl", [1, OCW], BF16))
        rows_s = E(nc.sbuf_tensor("rows_s", [1, 1024], F32))
        colA = E(nc.sbuf_tensor("colA", [128, 12], F32))
        catm = E(nc.sbuf_tensor("catm", [128, 64], F32))
        dttm = E(nc.sbuf_tensor("dttm", [128, 64], F32))
        datm = E(nc.sbuf_tensor("datm", [128, 64], F32))
        katm = E(nc.sbuf_tensor("katm", [128, 64], F32))
        atbc = E(nc.sbuf_tensor("atbc", [128, 64], F32))
        ectm = E(nc.sbuf_tensor("ectm", [128, 64], F32))
        cafm = E(nc.sbuf_tensor("cafm", [64, Q], F32))
        bmtm = E(nc.sbuf_tensor("bmtm", [128, 128], BF16))
        mmask = E(nc.sbuf_tensor("mmask", [128, 128], BF16))
        xstm = E(nc.sbuf_tensor("xstm", [128, 512], BF16))
        ytm = E(nc.sbuf_tensor("ytm", [128, 512], BF16))
        scf0 = E(nc.sbuf_tensor("scf0", [128, 512], F32))
        scf1 = E(nc.sbuf_tensor("scf1", [128, 512], F32))
        scf2 = E(nc.sbuf_tensor("scf2", [128, 512], F32))
        sqmid = E(nc.sbuf_tensor("sqmid", [128, 512], F32))
        scb0 = E(nc.sbuf_tensor("scb0", [128, 512], BF16))
        scb1 = E(nc.sbuf_tensor("scb1", [128, 512], BF16))
        scb2 = E(nc.sbuf_tensor("scb2", [128, 512], BF16))
        pha = E(nc.sbuf_tensor("pha", [128, 4096], F32))
        xqA = pha[:, 0:2048]
        xnbv = pha.bitcast(BF16)[:, 4096:8192]
        wslA = pha.bitcast(BF16)[:, 0:FC * 128]
        wslB = pha.bitcast(BF16)[:, FC * 128:2 * FC * 128]
        wcsA = pha.bitcast(BF16)[:, 0:4096]
        wcsB = pha.bitcast(BF16)[:, 4096:8192]
        icsv = pha.bitcast(BF16)[0:8, 4096:8192]
        ocs_sl = E(nc.sbuf_tensor("ocs_sl", [8, OCW], BF16))

        pA = E(nc.psum_tensor("pA", [128, 512], F32))
        pB = E(nc.psum_tensor("pB", [128, 512], F32))
        pT = E(nc.psum_tensor("pT", [128, 1024], BF16))
        pY = E(nc.psum_tensor("pY", [128, 512], F32))
        pS = E(nc.psum_tensor("pS", [128, 512], F32))
        pM = E(nc.psum_tensor("pM", [128, 512], F32))
        pU = E(nc.psum_tensor("pU", [128, 512], F32))
        pV = E(nc.psum_tensor("pV", [128, 512], F32))

        p = Prog(nc)
        for s in ("sx", "sw", "so", "sg", "sv", "sa", "sp",
                  "swc0", "swc1"):
            p.sem(s)
        for e in Prog.ENGS:
            p.sem("b_" + e)

        def TAP(t, src_ap):
            if t not in taps:
                return
            p.barrier()
            p.dma("sync", tap_d[t][:, :], src_ap, "so")
            p.wait("sync", "so")
            p.barrier()

        for dst, src in ((lnp, lnp_d), (nwp, nwp_d), (mupp, mupp_d), (cwp, cwp_d),
                         (dtb, dtb_d), (ident, identb_d), (identf, identf_d),
                         (tri, trib_d), (onesf, onesf_d), (onesrf, onesrf_d),
                         (onesrb, onesrb_d), (onescb, onescb_d)):
            p.dma("sync", dst[:, :], src[:, :], "sx")
        p.dma("sync", colA[0:64, 0:1], alog_d[:, :], "sx")
        p.dma("sync", rows_s[0:1, 0:64], drow_d[:, :], "sx")
        p.dma("gpsimd", gwr[:, :], gwb.rearrange("(k p) c -> p k c", p=128), "sw")
        p.wait("scalar", "sx")
        p.emit("scalar", lambda sc: sc.activation(nega[:, :], colA[0:64, 0:1], AF.Exp))
        va = p.tick("scalar", "sa")
        p.wait("vector", "sx")
        p.wait("vector", "sw")
        p.wait("vector", "sa", va)
        def init_v(ve):
            ve.tensor_scalar(nega[:, :], nega[:, :], 1e30, -1.0, ALU.min, ALU.mult)
            ve.reciprocal(lnrec[:, :], lnp[:, :])
            ve.memset(hbuf[:, :], 0.0)
        p.emit("vector", init_v)
        p.selfe("vector")
        for k in range(FC):
            p.emit("vector", lambda ve, k=k: ve.tensor_scalar(
                gwr[:, 8 * k:8 * k + 8], gwr[:, 8 * k:8 * k + 8],
                lnrec[:, k:k + 1], None, ALU.mult))
        vv = p.tick("vector", "sv")
        p.wait("tensor", "sv", vv)
        p.emit("tensor", lambda te: te.matmul(pU[:, 0:64], onesrf[:, :],
                                              rows_s[0:1, 0:64], start=True, stop=True))
        vt = p.tick("tensor", "sp")
        p.wait("vector", "sp", vt)
        p.emit("vector", lambda ve: ve.tensor_copy(dbc[:, :], pU[:, 0:64]))
        p.tick("vector", "sv")
        p.barrier()

        def emit_xcopy(mac, r0):
            for i in range(4):
                p.dma("sync", out_d[r0 + 256 * i:r0 + 256 * (i + 1), :],
                      xs4[mac, HALO + 256 * i:HALO + 256 * (i + 1), :], "so")

        def emit_ln(mac):
            p.barrier()
            tiles = [(128 * i, 128) for i in range(8)] + [(1024, 3)]
            for (g0, rn) in tiles:
                p.wait("scalar", "sp")
                for half in range(2):
                    vh = p.dma("sync", xqA[0:rn, :], xs4[mac, g0:g0 + rn, 2048 * half:2048 * (half + 1)], "sx")
                    p.wait("vector", "sx", vh)
                    p.wait("scalar", "sx", vh)
                    def redc(ve, rn=rn, half=half):
                        ve.reduce_sum(colA[0:rn, half:half + 1], xqA[0:rn, :], axis=AX)
                        ve.tensor_copy(xnbv[0:rn, 2048 * half:2048 * (half + 1)], xqA[0:rn, :])
                    p.emit("vector", redc)
                    p.selfe("vector")
                    vv = p.tick("vector", "sv")
                    p.wait("scalar", "sv", vv)
                    for k4 in range(4):
                        p.emit("scalar", lambda sc, rn=rn, half=half, k4=k4: sc.activation(
                            pV[0:rn, :], xqA[0:rn, 512 * k4:512 * (k4 + 1)], AF.Square,
                            accum_out=colA[0:rn, 2 + 4 * half + k4:3 + 4 * half + k4]))
                    va = p.tick("scalar", "sa")
                    p.wait("sync", "sa", va)
                    p.wait("sync", "sv", vv)
                p.wait("vector", "sa")
                for st_ in [
                    lambda ve, rn=rn: ve.tensor_tensor(colA[0:rn, 0:1], colA[0:rn, 0:1], colA[0:rn, 1:2], ALU.add),
                    lambda ve, rn=rn: ve.tensor_scalar(colA[0:rn, 0:1], colA[0:rn, 0:1], 1.0 / H, None, ALU.mult),
                    lambda ve, rn=rn: ve.reduce_sum(colA[0:rn, 3:4], colA[0:rn, 2:10], axis=AX),
                    lambda ve, rn=rn: ve.tensor_scalar(colA[0:rn, 3:4], colA[0:rn, 3:4], 1.0 / H, None, ALU.mult),
                    lambda ve, rn=rn: ve.tensor_tensor(colA[0:rn, 4:5], colA[0:rn, 0:1], colA[0:rn, 0:1], ALU.mult),
                    lambda ve, rn=rn: ve.tensor_tensor(colA[0:rn, 3:4], colA[0:rn, 3:4], colA[0:rn, 4:5], ALU.subtract),
                    lambda ve, rn=rn: ve.tensor_scalar(colA[0:rn, 4:5], colA[0:rn, 3:4], EPS, None, ALU.add),
                    lambda ve, rn=rn: ve.reciprocal(colA[0:rn, 5:6], colA[0:rn, 4:5]),
                    lambda ve, rn=rn: ve.tensor_tensor(colA[0:rn, 5:6], colA[0:rn, 3:4], colA[0:rn, 5:6], ALU.mult),
                    lambda ve, rn=rn: ve.tensor_scalar(colA[0:rn, 5:6], colA[0:rn, 5:6], EPS, None, ALU.add),
                ]:
                    p.emit("vector", st_)
                    p.selfe("vector")
                vv = p.tick("vector", "sv")
                p.wait("scalar", "sv", vv)
                p.emit("scalar", lambda sc, rn=rn: sc.activation(colA[0:rn, 4:5], colA[0:rn, 4:5], AF.Sqrt))
                p.emit("scalar", lambda sc, rn=rn: sc.activation(colA[0:rn, 5:6], colA[0:rn, 5:6], AF.Sqrt))
                va = p.tick("scalar", "sa")
                p.wait("vector", "sa", va)
                p.emit("vector", lambda ve, rn=rn: ve.reciprocal(colA[0:rn, 4:5], colA[0:rn, 4:5]))
                p.selfe("vector")
                p.emit("vector", lambda ve, rn=rn: ve.reciprocal(colA[0:rn, 5:6], colA[0:rn, 5:6]))
                p.selfe("vector")
                def xnb_(ve, rn=rn):
                    ve.tensor_scalar(xnbv[0:rn, :], xnbv[0:rn, :], colA[0:rn, 0:1], None, ALU.subtract)
                    ve.tensor_scalar(xnbv[0:rn, :], xnbv[0:rn, :], colA[0:rn, 4:5], None, ALU.mult)
                p.emit("vector", xnb_)
                vv = p.tick("vector", "sv")
                p.wait("tensor", "sv", vv)
                p.emit("tensor", lambda te, rn=rn: te.transpose(
                    pM[0:1, 0:rn], colA[0:rn, 5:6], identf[0:rn, 0:rn]))
                vt = p.tick("tensor", "sp")
                p.wait("vector", "sp", vt)
                p.emit("vector", lambda ve, rn=rn: ve.tensor_copy(rows_s[0:1, 0:rn], pM[0:1, 0:rn]))
                vv = p.tick("vector", "sv")
                p.wait("tensor", "sv", vv)
                p.emit("tensor", lambda te, rn=rn: te.matmul(
                    pU[:, 0:rn], onesrf[:, :], rows_s[0:1, 0:rn], start=True, stop=True))
                for g in range(4):
                    for j in range(8):
                        fc = 8 * g + j
                        p.emit("tensor", lambda te, fc=fc, j=j, rn=rn: te.transpose(
                            pT[:, 128 * j:128 * j + rn],
                            xnbv[0:rn, 128 * fc:128 * fc + 128], ident[0:rn, 0:rn]))
                    vt = p.tick("tensor", "sp")
                    p.wait("vector", "sp", vt)
                    for j in range(8):
                        fc = 8 * g + j
                        p.emit("vector", lambda ve, fc=fc, j=j, rn=rn, g0=g0: ve.tensor_scalar(
                            xnfm[:, TMH * fc + g0:TMH * fc + g0 + rn],
                            pT[:, 128 * j:128 * j + rn], lnp[:, fc:fc + 1], None, ALU.mult))
                    vv = p.tick("vector", "sv")
                    p.wait("tensor", "sv", vv)
                p.emit("vector", lambda ve, rn=rn, g0=g0: ve.tensor_copy(
                    r2rep[:, g0:g0 + rn], pU[:, 0:rn]))
                vv = p.tick("vector", "sv")
                p.wait("sync", "sv", vv)
                p.wait("scalar", "sv", vv)

        def emit_router(mac):
            for tt in range(8):
                g0 = HALO + 128 * tt
                for k in range(FC):
                    p.emit("tensor", lambda te, k=k, g0=g0: te.matmul(
                        pA[:, 0:8], xnfm[:, TMH * k + g0:TMH * k + g0 + 128],
                        gwr[:, 8 * k:8 * k + 8], start=(k == 0), stop=(k == FC - 1)))
                vt = p.tick("tensor", "sp")
                p.wait("vector", "sp", vt)
                p.emit("vector", lambda ve: ve.reduce_max(colA[:, 0:1], pA[:, 0:8], axis=AX))
                p.selfe("vector")
                p.emit("vector", lambda ve: ve.tensor_scalar(scf0[:, 0:8], pA[:, 0:8], colA[:, 0:1], None, ALU.subtract))
                p.selfe("vector")
                vv = p.tick("vector", "sv")
                p.wait("scalar", "sv", vv)
                p.emit("scalar", lambda sc: sc.activation(scf0[:, 0:8], scf0[:, 0:8], AF.Exp))
                va = p.tick("scalar", "sa")
                p.wait("vector", "sa", va)
                p.emit("vector", lambda ve: ve.reduce_sum(colA[:, 1:2], scf0[:, 0:8], axis=AX))
                p.selfe("vector")
                p.emit("vector", lambda ve: ve.reciprocal(colA[:, 1:2], colA[:, 1:2]))
                p.selfe("vector")
                p.emit("vector", lambda ve: ve.tensor_scalar(scb0[:, 0:8], scf0[:, 0:8], colA[:, 1:2], None, ALU.mult))
                p.selfe("vector")
                vv = p.tick("vector", "sv")
                p.wait("tensor", "sv", vv)
                p.emit("tensor", lambda te: te.transpose(pT[0:8, 0:128], scb0[:, 0:8], ident[:, :]))
                vt = p.tick("tensor", "sp")
                p.wait("vector", "sp", vt)
                p.emit("vector", lambda ve, tt=tt: ve.tensor_copy(
                    routfm[:, 128 * tt:128 * tt + 128], pT[0:8, 0:128]))
                vv = p.tick("vector", "sv")
                p.wait("tensor", "sv", vv)

        def emit_p1(mac):
            p.wait("gpsimd", "sv")
            p.wait("gpsimd", "sa")
            p.wait("gpsimd", "sp")
            cols = list(range(32, NCOL))
            lastmm = {}
            evtick = {}
            for i, m in enumerate(cols):
                buf = (wslA, wslB)[i % 2]
                w = COLW[m]
                if i >= 2:
                    p.wait("gpsimd", "sp", lastmm[i % 2])
                vw = p.dma("gpsimd", buf[:, 0:FC * w],
                           w1b[:, COLOFF[m]:COLOFF[m] + FC * w], "sw")
                p.wait("tensor", "sw", vw)
                for ti, (t0, tn) in enumerate(T3):
                    bi = (3 * i + ti) % 3
                    pp = (pA, pB, pV)[bi]
                    if bi in evtick:
                        p.wait("tensor", "sv", evtick[bi])
                    for k in range(FC):
                        p.emit("tensor", lambda te, pp=pp, buf=buf, k=k, w=w, t0=t0, tn=tn: te.matmul(
                            pp[0:w, 0:tn], buf[:, w * k:w * k + w],
                            xnfm[:, TMH * k + t0:TMH * k + t0 + tn],
                            start=(k == 0), stop=(k == FC - 1)))
                    vt = p.tick("tensor", "sp")
                    p.wait("vector", "sp", vt)
                    if m < 66:
                        cc = m - 32
                        p.emit("vector", lambda ve, pp=pp, cc=cc, m=m, t0=t0, tn=tn: ve.scalar_tensor_tensor(
                            arena[:, TMH * cc + t0:TMH * cc + t0 + tn], pp[0:128, 0:tn],
                            mupp[:, m:m + 1], r2rep[:, t0:t0 + tn], ALU.mult, ALU.mult))
                    else:
                        p.emit("vector", lambda ve, pp=pp, t0=t0, tn=tn: ve.scalar_tensor_tensor(
                            dtfb[:, t0:t0 + tn], pp[0:64, 0:tn],
                            mupp[0:64, 66:67], r2rep[0:64, t0:t0 + tn], ALU.mult, ALU.mult))
                    evtick[bi] = p.tick("vector", "sv")
                lastmm[i % 2] = p.val("sp")
            dtf = dtfb[:, :]
            dtp = dtpb[:, :]
            def sp1(ve):
                ve.tensor_scalar(dtf, dtf, dtb[:, :], None, ALU.add)
                ve.tensor_scalar(dtp, dtf, 0.0, None, ALU.max)
            p.emit("vector", sp1)
            vv = p.tick("vector", "sv")
            p.wait("scalar", "sv", vv)
            p.emit("scalar", lambda sc: sc.activation(dtf, dtf, AF.Abs))
            p.emit("scalar", lambda sc: sc.activation(dtf, dtf, AF.Exp, scale=-1.0))
            p.emit("scalar", lambda sc: sc.activation(dtf, dtf, AF.Ln, bias=1.0))
            va = p.tick("scalar", "sa")
            p.wait("vector", "sa", va)
            def sp2(ve):
                ve.tensor_tensor(dtp, dtp, dtf, ALU.add)
                ve.tensor_scalar(dtf, dtp, nega[:, :], None, ALU.mult)
            p.emit("vector", sp2)
            p.tick("vector", "sv")

        def emit_conv(mac):
            p.barrier(("vector", "scalar", "tensor"))
            silu_tick = {}
            for c in range(34):
                b0, b1 = (scf0, scf1) if c % 2 == 0 else (scf2, sqmid)
                if c >= 2:
                    p.wait("vector", "sa", silu_tick[c % 2])
                def cv(ve, c=c, b0=b0, b1=b1):
                    ve.tensor_scalar(b0[:, 0:512], arena[:, TMH * c:TMH * c + 512],
                                     cwp[:, 4 * c:4 * c + 1], None, ALU.mult)
                    ve.tensor_scalar(b1[:, 0:512], arena[:, TMH * c + 512:TMH * c + 1024],
                                     cwp[:, 4 * c:4 * c + 1], None, ALU.mult)
                    for k in (1, 2, 3):
                        ve.scalar_tensor_tensor(b0[:, 0:512], arena[:, TMH * c + k:TMH * c + k + 512],
                                                cwp[:, 4 * c + k:4 * c + k + 1], b0[:, 0:512],
                                                ALU.mult, ALU.add)
                        ve.scalar_tensor_tensor(b1[:, 0:512], arena[:, TMH * c + 512 + k:TMH * c + 512 + k + 512],
                                                cwp[:, 4 * c + k:4 * c + k + 1], b1[:, 0:512],
                                                ALU.mult, ALU.add)
                p.emit("vector", cv)
                vv = p.tick("vector", "sv")
                p.wait("scalar", "sv", vv)
                p.emit("scalar", lambda sc, c=c, b0=b0: sc.activation(
                    arena[:, TMH * c:TMH * c + 512], b0[:, 0:512], AF.Silu))
                p.emit("scalar", lambda sc, c=c, b1=b1: sc.activation(
                    arena[:, TMH * c + 512:TMH * c + 1024], b1[:, 0:512], AF.Silu))
                silu_tick[c % 2] = p.tick("scalar", "sa")
            p.wait("vector", "sa")

        def emit_scan(mac, prefix):
            dtf = dtfb[:, :]
            dtp = dtpb[:, :]
            p.barrier(("vector", "scalar", "tensor"))
            sqb = sqmid.bitcast(BF16)
            vcopy = t_yintra = t_smm = a_exp = a_yevscale = v_yevTT = None
            for q in range(NCH):
                t0 = q * Q
                d0 = HALO + q * Q
                p.emit("vector", lambda ve, d0=d0: ve.tensor_tensor_scan(
                    cafm[:, :], dtf[:, d0:d0 + Q], dtf[:, d0:d0 + Q], 0.0, ALU.add, ALU.bypass))
                vv = p.tick("vector", "sv")
                p.wait("tensor", "sv", vv)
                p.emit("tensor", lambda te: te.transpose(pM[:, 0:64], cafm[:, :], identf[0:64, 0:64]))
                p.emit("tensor", lambda te, d0=d0: te.transpose(pM[:, 64:128], dtf[:, d0:d0 + Q], identf[0:64, 0:64]))
                p.emit("tensor", lambda te, d0=d0: te.transpose(pM[:, 128:192], dtp[:, d0:d0 + Q], identf[0:64, 0:64]))
                p.emit("tensor", lambda te, t0=t0: te.transpose(
                    pT[:, 0:128], arena[:, TMH * 32 + t0:TMH * 32 + t0 + Q], ident[:, :]))
                vt = p.tick("tensor", "sp")
                p.wait("vector", "sp", vt)
                def ev_t(ve):
                    ve.tensor_copy(catm[:, :], pM[:, 0:64])
                    ve.tensor_copy(datm[:, :], pM[:, 64:128])
                    ve.tensor_copy(dttm[:, :], pM[:, 128:192])
                    ve.tensor_copy(bmtm[:, :], pT[:, 0:128])
                p.emit("vector", ev_t)
                vv = p.tick("vector", "sv")
                p.wait("tensor", "sv", vv)
                if a_yevscale is not None:
                    p.wait("tensor", "sa", a_yevscale)
                p.emit("tensor", lambda te: te.matmul(pU[:, 0:64], onesf[:, :], datm[:, :],
                                                      start=True, stop=True))
                vt = p.tick("tensor", "sp")
                p.wait("vector", "sp", vt)
                p.emit("vector", lambda ve: ve.tensor_tensor(katm[:, :], pU[:, 0:64], catm[:, :], ALU.subtract))
                vv = p.tick("vector", "sv")
                p.wait("scalar", "sv", vv)
                p.emit("scalar", lambda sc: sc.activation(atbc[:, :], pU[:, 0:64], AF.Exp))
                p.emit("scalar", lambda sc: sc.activation(katm[:, :], katm[:, :], AF.Exp))
                if not prefix:
                    p.emit("scalar", lambda sc: sc.activation(ectm[:, :], catm[:, :], AF.Exp))
                va = p.tick("scalar", "sa")
                p.wait("vector", "sa", va)
                p.wait("tensor", "sa", va)
                if not prefix:
                    p.emit("tensor", lambda te, t0=t0: te.matmul(
                        pM[:, 0:128], arena[:, TMH * 32 + t0:TMH * 32 + t0 + Q],
                        arena[:, TMH * 33 + t0:TMH * 33 + t0 + Q], start=True, stop=True))
                    vt = p.tick("tensor", "sp")
                    p.wait("vector", "sp", vt)
                    p.emit("vector", lambda ve: ve.tensor_tensor(mmask[:, :], pM[:, 0:128], tri[:, :], ALU.mult))
                    vv = p.tick("vector", "sv")
                    p.wait("tensor", "sv", vv)
                for g in range(8):
                    if vcopy is not None:
                        p.wait("tensor", "sv", vcopy)
                    for j in range(4):
                        c = 4 * g + j
                        p.emit("tensor", lambda te, c=c, j=j, t0=t0: te.transpose(
                            pT[:, 128 * j:128 * (j + 1)],
                            arena[:, TMH * c + t0:TMH * c + t0 + Q], ident[:, :]))
                    vt = p.tick("tensor", "sp")
                    if t_yintra is not None:
                        p.wait("vector", "sp", t_yintra)
                    elif t_smm is not None:
                        p.wait("vector", "sp", t_smm)
                    p.wait("vector", "sp", vt)
                    p.emit("vector", lambda ve: ve.tensor_copy(xstm[:, :], pT[:, 0:512]))
                    if not prefix:
                        p.emit("vector", lambda ve, g=g: ve.tensor_copy(
                            scb2[:, :], hbuf[:, 512 * g:512 * (g + 1)]))
                    for j in range(8):
                        r = 8 * g + j
                        p.emit("vector", lambda ve, r=r, j=j: ve.tensor_scalar(
                            scb0[:, 64 * j:64 * (j + 1)], xstm[:, 64 * j:64 * (j + 1)],
                            dttm[:, r:r + 1], None, ALU.mult))
                    vcopy = p.tick("vector", "sv")
                    p.wait("tensor", "sv", vcopy)
                    if not prefix:
                        if a_exp is not None:
                            p.wait("vector", "sa", a_exp)
                        def dg8(ve, g=g):
                            for jj in range(4):
                                r = 8 * g + jj
                                ve.tensor_scalar(scf0[:, 128 * jj:128 * (jj + 1)],
                                                 identf[:, :], catm[:, r:r + 1], None, ALU.mult)
                            for jj in range(4):
                                r = 8 * g + 4 + jj
                                ve.tensor_scalar(scf1[:, 128 * jj:128 * (jj + 1)],
                                                 identf[:, :], catm[:, r:r + 1], None, ALU.mult)
                        p.emit("vector", dg8)
                        v_dg = p.tick("vector", "sv")
                        p.emit("tensor", lambda te, g=g, t0=t0: te.matmul(
                            pU[:, :], arena[:, TMH * 33 + t0:TMH * 33 + t0 + Q],
                            scb2[:, :], start=True, stop=True))
                        vys = p.tick("tensor", "sp")
                        p.wait("tensor", "sv", v_dg)
                        p.emit("tensor", lambda te: te.matmul(pB[:, :], onesf[:, :],
                                                              scf0[:, :], start=True, stop=True))
                        p.emit("tensor", lambda te: te.matmul(pV[:, :], onesf[:, :],
                                                              scf1[:, :], start=True, stop=True))
                        t_rr = p.tick("tensor", "sp")
                        p.wait("scalar", "sv", vcopy)
                        p.wait("scalar", "sp", vys)
                        def xw8(sc, g=g):
                            for j in range(8):
                                r = 8 * g + j
                                sc.activation(scb2[:, 64 * j:64 * (j + 1)],
                                              scb0[:, 64 * j:64 * (j + 1)], AF.Copy,
                                              scale=katm[:, r:r + 1])
                        p.emit("scalar", xw8)
                        a_xw = p.tick("scalar", "sa")
                        if v_yevTT is not None:
                            p.wait("scalar", "sv", v_yevTT)
                        def ys8(sc, g=g):
                            for j in range(8):
                                r = 8 * g + j
                                sc.activation(scf2[:, 64 * j:64 * (j + 1)],
                                              pU[:, 64 * j:64 * (j + 1)], AF.Copy,
                                              scale=ectm[:, r:r + 1])
                        p.emit("scalar", ys8)
                        a_yevscale = p.tick("scalar", "sa")
                        p.wait("tensor", "sa", a_xw)
                        p.emit("tensor", lambda te: te.matmul(pS[:, :], bmtm[:, :], scb2[:, :],
                                                              start=True, stop=True))
                        t_smm = p.tick("tensor", "sp")
                        p.wait("vector", "sp", t_rr)
                        def g28(ve, g=g):
                            for jj in range(4):
                                r = 8 * g + jj
                                ve.tensor_scalar(scf0[:, 128 * jj:128 * (jj + 1)],
                                                 pB[:, 128 * jj:128 * (jj + 1)],
                                                 catm[:, r:r + 1], 0.0, ALU.subtract, ALU.min)
                            for jj in range(4):
                                r = 8 * g + 4 + jj
                                ve.tensor_scalar(scf1[:, 128 * jj:128 * (jj + 1)],
                                                 pV[:, 128 * jj:128 * (jj + 1)],
                                                 catm[:, r:r + 1], 0.0, ALU.subtract, ALU.min)
                        p.emit("vector", g28)
                        v_g2 = p.tick("vector", "sv")
                        p.wait("scalar", "sv", v_g2)
                        if t_yintra is not None:
                            p.wait("scalar", "sp", t_yintra)
                        p.emit("scalar", lambda sc: sc.activation(sqb[:, 0:512], scf0[:, :], AF.Exp))
                        p.emit("scalar", lambda sc: sc.activation(sqb[:, 512:1024], scf1[:, :], AF.Exp))
                        a_exp = p.tick("scalar", "sa")
                        p.wait("vector", "sa", a_exp)
                        def gmb2(ve):
                            ve.tensor_tensor(
                                sqb[:, 0:512].rearrange("p (a b) -> p a b", a=4),
                                sqb[:, 0:512].rearrange("p (a b) -> p a b", a=4),
                                mmask[:, None, :].broadcast_to([128, 4, 128]), ALU.mult)
                            ve.tensor_tensor(
                                sqb[:, 512:1024].rearrange("p (a b) -> p a b", a=4),
                                sqb[:, 512:1024].rearrange("p (a b) -> p a b", a=4),
                                mmask[:, None, :].broadcast_to([128, 4, 128]), ALU.mult)
                        p.emit("vector", gmb2)
                        v_gmb = p.tick("vector", "sv")
                        p.wait("tensor", "sv", v_gmb)
                        for j in range(8):
                            p.emit("tensor", lambda te, j=j: te.matmul(
                                pY[:, 64 * j:64 * (j + 1)], sqb[:, 128 * j:128 * (j + 1)],
                                scb0[:, 64 * j:64 * (j + 1)],
                                start=(j == 0), stop=(j == 7)))
                        t_yintra = p.tick("tensor", "sp")
                    else:
                        p.wait("scalar", "sv", vcopy)
                        if t_smm is not None:
                            p.wait("scalar", "sp", t_smm)
                        def xw8(sc, g=g):
                            for j in range(8):
                                r = 8 * g + j
                                sc.activation(scb2[:, 64 * j:64 * (j + 1)],
                                              scb0[:, 64 * j:64 * (j + 1)], AF.Copy,
                                              scale=katm[:, r:r + 1])
                        p.emit("scalar", xw8)
                        a_xw = p.tick("scalar", "sa")
                        p.wait("tensor", "sa", a_xw)
                        p.emit("tensor", lambda te: te.matmul(pS[:, :], bmtm[:, :], scb2[:, :],
                                                              start=True, stop=True))
                        t_smm = p.tick("tensor", "sp")
                    p.wait("vector", "sp", t_smm)
                    def hup(ve, g=g):
                        for j in range(8):
                            r = 8 * g + j
                            ve.scalar_tensor_tensor(hbuf[:, 64 * r:64 * (r + 1)],
                                                    hbuf[:, 64 * r:64 * (r + 1)],
                                                    atbc[:, r:r + 1], pS[:, 64 * j:64 * (j + 1)],
                                                    ALU.mult, ALU.add)
                    p.emit("vector", hup)
                    if not prefix:
                        p.wait("vector", "sa", a_yevscale)
                        def yev(ve, g=g):
                            for j in range(8):
                                r = 8 * g + j
                                ve.scalar_tensor_tensor(scf2[:, 64 * j:64 * (j + 1)],
                                                        xstm[:, 64 * j:64 * (j + 1)],
                                                        dbc[:, r:r + 1], scf2[:, 64 * j:64 * (j + 1)],
                                                        ALU.mult, ALU.add)
                        p.emit("vector", yev)
                        p.wait("vector", "sp", t_yintra)
                        p.emit("vector", lambda ve: ve.tensor_tensor(ytm[:, :], pY[:, :], scf2[:, :], ALU.add))
                        v_yevTT = p.tick("vector", "sv")
                        p.wait("tensor", "sv", v_yevTT)
                        pMb = pM.bitcast(BF16)
                        for j in range(4):
                            p.emit("tensor", lambda te, j=j, pMb=pMb: te.transpose(
                                pMb[:, 128 * j:128 * (j + 1)], ytm[:, 128 * j:128 * (j + 1)],
                                ident[:, :]))
                        t_ytr = p.tick("tensor", "sp")
                        p.wait("vector", "sp", t_ytr)
                        def yst(ve, g=g, t0=t0, pMb=pMb):
                            for j in range(4):
                                c = 4 * g + j
                                ve.tensor_copy(arena[:, TMH * c + t0:TMH * c + t0 + Q],
                                               pMb[:, 128 * j:128 * (j + 1)])
                        p.emit("vector", yst)
                    p.tick("vector", "sv")

        def emit_z(mac):
            p.barrier(("vector", "scalar", "tensor", "gpsimd"))
            lastmm = {}
            evtick = {}
            for m in range(32):
                buf = (wslA, wslB)[m % 2]
                if m >= 2:
                    p.wait("gpsimd", "sp", lastmm[m % 2])
                vw = p.dma("gpsimd", buf[:, 0:FC * 128], w1b[:, COLOFF[m]:COLOFF[m] + FC * 128], "sw")
                p.wait("tensor", "sw", vw)
                for tt in range(2):
                    t0 = HALO + 512 * tt
                    bi = (2 * m + tt) % 2
                    pp = (pA, pB)[bi]
                    par = (2 * m + tt) % 2
                    sf0 = (scf0, scf1)[par]
                    sq_ = (scb0, scb2)[par]
                    if bi in evtick:
                        p.wait("tensor", "sv", evtick[bi])
                    for k in range(FC):
                        p.emit("tensor", lambda te, pp=pp, buf=buf, k=k, t0=t0: te.matmul(
                            pp[:, :], buf[:, 128 * k:128 * (k + 1)],
                            xnfm[:, TMH * k + t0:TMH * k + t0 + 512],
                            start=(k == 0), stop=(k == FC - 1)))
                    vt = p.tick("tensor", "sp")
                    p.wait("vector", "sp", vt)
                    p.emit("vector", lambda ve, pp=pp, m=m, t0=t0, sf0=sf0: ve.scalar_tensor_tensor(
                        sf0[:, :], pp[:, :], mupp[:, m:m + 1], r2rep[:, t0:t0 + 512],
                        ALU.mult, ALU.mult))
                    evtick[bi] = p.tick("vector", "sv")
                    p.wait("scalar", "sv", evtick[bi])
                    p.emit("scalar", lambda sc, sf0=sf0: sc.activation(
                        scf2[:, :], sf0[:, :], AF.Silu))
                    va = p.tick("scalar", "sa")
                    p.wait("vector", "sa", va)
                    lt = 512 * tt
                    p.emit("vector", lambda ve, m=m, lt=lt: ve.tensor_tensor(
                        arena[:, TMH * m + lt:TMH * m + lt + 512],
                        arena[:, TMH * m + lt:TMH * m + lt + 512], scf2[:, :], ALU.mult))
                    p.tick("vector", "sv")
                lastmm[m % 2] = p.val("sp")

        def emit_rstd3(mac):
            p.barrier(("vector", "scalar", "tensor"))
            sqticks = {}
            for c in range(FC):
                for tt in range(2):
                    lt = 512 * tt
                    sq_ = (scb0, scb2)[(2 * c + tt) % 2]
                    if (c, tt) not in ((0, 0), (0, 1)):
                        p.wait("scalar", "sp", sqticks[(2 * c + tt) % 2])
                    p.emit("scalar", lambda sc, c=c, lt=lt, sq_=sq_: sc.activation(
                        sq_[:, :], arena[:, TMH * c + lt:TMH * c + lt + 512], AF.Square))
                    va = p.tick("scalar", "sa")
                    p.wait("tensor", "sa", va)
                    pst = (pU, pV)[tt]
                    p.emit("tensor", lambda te, pst=pst, c=c, sq_=sq_: te.matmul(
                        pst[0:1, :], onescb[:, :], sq_[:, :],
                        start=(c == 0), stop=(c == FC - 1)))
                    sqticks[(2 * c + tt) % 2] = p.tick("tensor", "sp")
            p.wait("vector", "sp")
            def r3a(ve):
                ve.tensor_copy(rows_s[0:1, 0:512], pU[0:1, :])
                ve.tensor_copy(rows_s[0:1, 512:1024], pV[0:1, :])
                ve.tensor_scalar(rows_s[0:1, 0:1024], rows_s[0:1, 0:1024], 1.0 / DS, EPS,
                                 ALU.mult, ALU.add)
            p.emit("vector", r3a)
            vv = p.tick("vector", "sv")
            p.wait("scalar", "sv", vv)
            p.emit("scalar", lambda sc: sc.activation(rows_s[0:1, 0:1024], rows_s[0:1, 0:1024], AF.Sqrt))
            va = p.tick("scalar", "sa")
            p.wait("vector", "sa", va)
            p.emit("vector", lambda ve: ve.reciprocal(rows_s[0:1, 0:1024], rows_s[0:1, 0:1024]))
            vv = p.tick("vector", "sv")
            p.wait("tensor", "sv", vv)
            for tt in range(2):
                p.emit("tensor", lambda te, tt=tt: te.matmul(
                    pM[:, :], onesrf[:, :], rows_s[0:1, 512 * tt:512 * (tt + 1)],
                    start=True, stop=True))
                vt = p.tick("tensor", "sp")
                p.wait("vector", "sp", vt)
                p.emit("vector", lambda ve, tt=tt: ve.tensor_copy(
                    r2rep[:, 512 * tt:512 * (tt + 1)], pM[:, :]))
                vv = p.tick("vector", "sv")
                p.wait("tensor", "sv", vv)

        def emit_yn(mac):
            for c in range(FC):
                for lt in (0, 512):
                    p.emit("vector", lambda ve, c=c, lt=lt: ve.scalar_tensor_tensor(
                        arena[:, TMH * c + lt:TMH * c + lt + 512],
                        arena[:, TMH * c + lt:TMH * c + lt + 512],
                        nwp[:, c:c + 1], r2rep[:, lt:lt + 512], ALU.mult, ALU.mult))
            p.tick("vector", "sv")

        def emit_xin(mac):
            p.wait("gpsimd", "sv")
            p.wait("gpsimd", "sp")
            vw = p.dma("gpsimd", icsv, icsb_d[:, :], "sw")
            p.wait("tensor", "sw", vw)
            for fc in range(FC):
                for tt in range(2):
                    pp = (pA, pB)[tt]
                    p.emit("tensor", lambda te, pp=pp, fc=fc, tt=tt: te.matmul(
                        pp[:, :], icsv[:, 128 * fc:128 * (fc + 1)],
                        routfm[:, 512 * tt:512 * (tt + 1)], start=True, stop=True))
                    vt = p.tick("tensor", "sp")
                    p.wait("vector", "sp", vt)
                    g0 = HALO + 512 * tt
                    p.emit("vector", lambda ve, pp=pp, fc=fc, g0=g0: ve.scalar_tensor_tensor(
                        xnfm[:, TMH * fc + g0:TMH * fc + g0 + 512], pp[:, :],
                        lnrec[:, fc:fc + 1], xnfm[:, TMH * fc + g0:TMH * fc + g0 + 512],
                        ALU.mult, ALU.mult))
                    vv = p.tick("vector", "sv")
                    p.wait("tensor", "sv", vv)

        def emit_c(mac, ffn, r0):
            p.barrier(("vector", "scalar", "tensor", "gpsimd"))
            p.wait("gpsimd", "so")
            p.wait("gpsimd", "sg")
            wsrc = wtb if ffn else w2b
            coff = HALO if ffn else 0
            srcbase = xnfm if ffn else arena
            banks = [pA[:, :], pB[:, :], pY[:, :], pS[:, :], pM[:, :], pU[:, :], pV[:, :], pT.bitcast(F32)[:, 0:512]]
            lastuse = {}
            bank_ev = {}
            NSL = NOC * NKQ
            slab_tick = {}

            def issue_slab(n):
                buf = (wcsA, wcsB)[n % 2]
                if n >= 2:
                    p.wait("gpsimd", "sp", lastuse[n % 2])
                slab_tick[n] = p.dma("gpsimd", buf[:, :], wsrc[n * 128:(n + 1) * 128, :],
                                     "swc%d" % (n % 2))

            issue_slab(0)
            issue_slab(1)
            for j in range(NOC):
                if ffn:
                    p.wait("gpsimd", "sp")
                    vw = p.dma("gpsimd", ocs_sl[:, :], ocsb_d[:, OCW * j:OCW * (j + 1)], "sw")
                    vw = p.dma("gpsimd", bias_sl[:, :], biasr_d[0:1, OCW * j:OCW * (j + 1)], "sw")
                    p.wait("tensor", "sw", vw)
                for kq in range(NKQ):
                    n = j * NKQ + kq
                    buf = (wcsA, wcsB)[n % 2]
                    vw = slab_tick[n]
                    sws = "swc%d" % (n % 2)
                    if ffn:
                        p.wait("vector", sws, vw)
                        p.emit("vector", lambda ve, buf=buf: ve.tensor_scalar(
                            buf.bitcast(U16), buf.bitcast(U16), 0x8000, 0x3F80,
                            ALU.bitwise_and, ALU.bitwise_or))
                        vv = p.tick("vector", "sv")
                        p.wait("tensor", "sv", vv)
                    else:
                        p.wait("tensor", sws, vw)
                    for tt in range(8):
                        if kq == 0 and tt in bank_ev:
                            p.wait("tensor", bank_ev[tt][0], bank_ev[tt][1])
                        for kin in range(8):
                            k = kq * 8 + kin
                            g0 = coff + 128 * tt
                            p.emit("tensor", lambda te, bank=banks[tt], buf=buf, kin=kin, k=k, g0=g0: te.matmul(
                                bank,
                                srcbase[:, TMH * k + g0:TMH * k + g0 + 128],
                                buf[:, 512 * kin:512 * (kin + 1)],
                                start=(k == 0),
                                stop=(k == 31 and not ffn)))
                    vt = p.tick("tensor", "sp")
                    lastuse[n % 2] = vt
                    if n + 2 < NSL:
                        issue_slab(n + 2)
                if ffn:
                    for tt in range(8):
                        p.emit("tensor", lambda te, tt=tt: te.matmul(
                            banks[tt], onesrb[:, :], bias_sl[0:1, :],
                            start=False, stop=True))
                vt = p.tick("tensor", "sp")
                stags = (scf0, scf1)
                dmaticks = {}
                for tt in range(8):
                    stag = stags[tt % 2]
                    peng = "scalar" if (tt % 2 == 0 and not ffn) else "vector"
                    if tt >= 2:
                        p.wait(peng, "sg", dmaticks[tt - 2])
                    if tt == 0:
                        p.wait("vector", "sp", vt)
                        p.wait("scalar", "sp", vt)
                    if tt % 2 == 0 and not ffn:
                        p.emit("scalar", lambda sc, tt=tt, stag=stag: sc.activation(
                            stag[:, :], banks[tt], AF.Copy))
                        esem, ev = "sa", p.tick("scalar", "sa")
                    else:
                        p.emit("vector", lambda ve, tt=tt, stag=stag: ve.tensor_copy(stag[:, :], banks[tt]))
                        esem, ev = "sv", p.tick("vector", "sv")
                    bank_ev[tt] = (esem, ev)
                    if ffn:
                        p.wait("tensor", esem, ev)
                        p.emit("tensor", lambda te, tt=tt, j=j: te.matmul(
                            banks[tt], routfm[:, 128 * tt:128 * tt + 128],
                            ocs_sl[:, :], start=True, stop=True))
                        vt2 = p.tick("tensor", "sp")
                        p.wait("vector", "sp", vt2)
                        def fev(ve, tt=tt, stag=stag):
                            ve.tensor_copy(scf2[:, :], banks[tt])
                            ve.tensor_tensor(stag[:, :], stag[:, :], scf2[:, :], ALU.mult)
                        p.emit("vector", fev)
                        esem, ev = "sv", p.tick("vector", "sv")
                        bank_ev[tt] = (esem, ev)
                    p.wait("gpsimd", esem, ev)
                    dmaticks[tt] = p.dma("gpsimd",
                                         out_d[r0 + 128 * tt:r0 + 128 * (tt + 1), OCW * j:OCW * (j + 1)],
                                         stag[:, :], "sg", accum_op=ALU.add)
                p.wait("vector", "sg", dmaticks[6])
                p.wait("vector", "sg", dmaticks[7])
                p.wait("scalar", "sg", dmaticks[6])

        n_main = 0
        for mac in range(n_mac):
            prefix = modes[mac] == "prefix"
            r0 = n_main * TM
            if not prefix:
                n_main += 1
                emit_xcopy(mac, r0)
            emit_ln(mac)
            TAP(f"xnfm@{mac}", xnfm[:, :])
            TAP(f"r2rep@{mac}", r2rep[:, :])
            TAP(f"colA@{mac}", colA[:, :])
            TAP(f"rowss@{mac}", rows_s[:, :])
            if stop == "ln":
                break
            if not prefix:
                emit_router(mac)
                TAP(f"routfm@{mac}", routfm[:, :])
            if stop == "router":
                break
            emit_p1(mac)
            TAP(f"arena@{mac}p1", arena[:, :])
            TAP(f"dtf@{mac}", dtfb[:, :])
            TAP(f"dtp@{mac}", dtpb[:, :])
            if stop == "p1":
                break
            emit_conv(mac)
            TAP(f"arena@{mac}cv", arena[:, :])
            if stop == "conv":
                break
            emit_scan(mac, prefix)
            TAP(f"hbuf@{mac}", hbuf[:, :])
            if stop == "scan":
                break
            if not prefix:
                TAP(f"arena@{mac}y", arena[:, :])
                emit_z(mac)
                emit_rstd3(mac)
                TAP(f"arena@{mac}z", arena[:, :])
                if stop == "z":
                    break
                emit_yn(mac)
                emit_xin(mac)
                if stop == "xin":
                    break
                emit_c(mac, ffn=False, r0=r0)
                if stop == "c1":
                    break
                emit_c(mac, ffn=True, r0=r0)
        p.barrier()
        p.wait("sync", "so")
        p.wait("gpsimd", "so")
        p.wait("gpsimd", "sg")
        p.wait("gpsimd", "sw")

        with nc.Block() as block:
            @block.sync
            def _(e):
                p.replay("sync", e)

            @block.gpsimd
            def _(e):
                p.replay("gpsimd", e)

            @block.vector
            def _(e):
                p.replay("vector", e)

            @block.scalar
            def _(e):
                p.replay("scalar", e)

            @block.tensor
            def _(e):
                p.replay("tensor", e)

    return nc


def kernel(**inputs):
    sh = host_shared(inputs)
    nc = build()
    in_maps = []
    for c in range(NCORES):
        m = dict(sh)
        m.update(host_core_inputs(inputs, c))
        in_maps.append(m)
    res = run_bass_kernel_spmd(nc, in_maps, core_ids=list(range(NCORES)))
    out = np.zeros((4, 4096, H), np.float32)
    for c in range(NCORES):
        b, h = c // 2, c % 2
        out[b, 2048 * h:2048 * (h + 1)] = res.results[c]["out"]
    return out
